# version 12
# baseline (speedup 1.0000x reference)
import math
import numpy as np
import ml_dtypes
from contextlib import ExitStack

import concourse.bass as bass
import concourse.mybir as mybir
from concourse.bass_utils import run_bass_kernel_spmd

F32 = mybir.dt.float32
BF16 = mybir.dt.bfloat16
AF = mybir.ActivationFunctionType
ALU = mybir.AluOpType
AX = mybir.AxisListType

D = 2048
S = 2048
TOK = 1024
NTC = 16
NTO = 8
Dh = 64
INTER = 8192
EPS = 1e-5
LAMBDA_INIT = 0.8 - 0.6 * math.exp(-0.3 * 2)
ROPE_THETA = 10000.0
SCALING = Dh ** -0.5

ENGINES = ["sync", "gpsimd", "tensor", "vector", "scalar"]


class Plan:

    def __init__(self):
        self.prog = {e: [] for e in ENGINES}
        self.cnt = {}
        self.rel = {}

    def sem(self, name):
        self.cnt[name] = 0

    def wait(self, eng, sem, n=None):
        v = self.cnt[sem] if n is None else n
        if v > 0:
            self.prog[eng].append(("w", sem, v))

    def op(self, eng, fn, inc=None, k=1):
        tgt = None
        if inc is not None:
            self.cnt[inc] += k
            tgt = (inc, k)
        self.prog[eng].append(("o", fn, tgt))

    def acquire(self, eng, res):
        for sem, n in self.rel.get(res, ()):
            self.wait(eng, sem, n)

    def release(self, res, sem, n=None):
        self.rel[res] = [(sem, self.cnt[sem] if n is None else n)]

    def release_also(self, res, sem, n=None):
        self.rel.setdefault(res, []).append(
            (sem, self.cnt[sem] if n is None else n))


def build_nc():
    nc = bass.Bass("TRN2", target_bir_lowering=False)

    def din(name, shape, dt=F32):
        return nc.dram_tensor(name, shape, dt, kind="ExternalInput").ap()

    x_in = din("x", [S, D])
    xq_in = din("xq", [TOK, D])
    wq_t = din("wq_t", [4, 128, 16, 512])
    wk_t = din("wk_t", [2, 128, 16, 512])
    wv_t = din("wv_t", [2, 128, 16, 512])
    wo_t = din("wo_t", [4, 128, 16, 512])
    wgu_t = din("wgu_t", [128, 128, 16, 128])
    wd_t = din("wd_t", [4, 4, 128, 16, 512])
    qcos_in = din("qcos", [TOK, Dh])
    qsin_in = din("qsin", [TOK, Dh])
    kcos_in = din("kcos", [S, Dh])
    ksin_in = din("ksin", [S, Dh])
    mask_in = din("mask", [128, 8 * 2048], BF16)
    ident_in = din("ident", [128, 128])
    lq1_in = din("lq1", [1, Dh])
    lk1_in = din("lk1", [1, Dh])
    lq2_in = din("lq2", [1, Dh])
    lk2_in = din("lk2", [1, Dh])
    subw_in = din("subw", [1, 2 * Dh])

    s_dram = nc.dram_tensor("s_scr", [TOK, D], F32, kind="Internal").ap()
    negb_dram = nc.dram_tensor("negb_scr", [8, 32, 128], BF16,
                               kind="Internal").ap()
    h_dram = nc.dram_tensor("h_scr", [64, 128, 1024], BF16,
                            kind="Internal").ap()
    wd_bf = nc.dram_tensor("wd_bf", [4, 4, 128, 16, 512], BF16,
                           kind="Internal").ap()
    xn_dram = nc.dram_tensor("xn_scr", [16, 128, 16, 128], BF16,
                             kind="Internal").ap()
    wo_bf = nc.dram_tensor("wo_bf", [4, 128, 16, 512], BF16,
                           kind="Internal").ap()
    attn_dram = nc.dram_tensor("attn_scr", [TOK, D], BF16,
                               kind="Internal").ap()
    out_d = nc.dram_tensor("out", [TOK, D], F32, kind="ExternalOutput").ap()

    es = ExitStack()

    def sb(stk, name, shape, dt, side):
        return stk.enter_context(nc.sbuf_tensor(name, shape, dt, side=side))

    id_bf = sb(es, "id_bf", [128, 128], BF16, "left")
    id_f = sb(es, "id_f", [128, 128], F32, "left")
    ones_f = sb(es, "ones_f", [1, 128], F32, "left")
    ones_bf = sb(es, "ones_bf", [128, 128], BF16, "left")
    lamn_col = sb(es, "lamn_col", [128, 1], F32, "left")
    w_bcast = sb(es, "w_bcast", [128, 128], F32, "left")
    b_sb = sb(es, "b_sb", [128, 256], F32, "left")
    kdmax_f = sb(es, "kdmax_f", [128, 8], F32, "left")
    kdmax_b = sb(es, "kdmax_b", [128, 8], BF16, "left")
    small = sb(es, "small", [1, 640], F32, "left")
    epscol = sb(es, "epscol", [128, 1], F32, "left")
    onecol = sb(es, "onecol", [128, 1], F32, "left")
    h_stg = [sb(es, f"h_stg{i}", [128, 512], BF16, "left") for i in range(2)]
    stage = [sb(es, f"stage{i}", [128, 512], F32, "left") for i in range(2)]
    xn_own = sb(es, "xn_own", [128, 16 * 1024], BF16, "left")

    arena = es.enter_context(nc.psum_tensor("arena", [128, 4096], F32))

    sems = {}
    for sname in ["dT", "dID", "dX0", "dX1", "dM0", "dM1", "dSB0", "dSB1",
                  "dWB0", "dWB1", "dGU0", "dGU1", "dGU2", "dWD0", "dWD1",
                  "dXN0", "dXN1",
                  "dST0", "dST1", "dH0", "dH1", "dWC", "dAT0", "dAT1", "dAT2",
                  "dAT3", "tS", "vS", "sS"]:
        sems[sname] = es.enter_context(nc.semaphore(sname))

    P = Plan()
    for sname in sems:
        P.sem(sname)

    def bank(i, w=512):
        return arena[:, i * 512: i * 512 + w]

    def bankc(i, off, w):
        return arena[:, i * 512 + off: i * 512 + off + w]

    P.op("sync", lambda e: e.dma_start(out=id_f[:], in_=ident_in), "dT", 16)
    P.op("sync", lambda e: e.dma_start(out=small[:, 0:64], in_=lq1_in),
         "dT", 16)
    P.op("sync", lambda e: e.dma_start(out=small[:, 64:128], in_=lk1_in),
         "dT", 16)
    P.op("sync", lambda e: e.dma_start(out=small[:, 128:192], in_=lq2_in),
         "dT", 16)
    P.op("sync", lambda e: e.dma_start(out=small[:, 192:256], in_=lk2_in),
         "dT", 16)
    P.op("sync", lambda e: e.dma_start(out=small[:, 256:384], in_=subw_in),
         "dT", 16)
    P.op("gpsimd", lambda e: e.dma_start(out=id_bf[:], in_=ident_in),
         "dID", 16)


    esKVQ = ExitStack()
    k_fm = sb(esKVQ, "k_fm", [128, 8 * 2048], BF16, "left")
    v_sb = sb(esKVQ, "v_sb", [128, 8 * 16 * 132], BF16, "left")
    esP1 = ExitStack()
    xbuf = [sb(esP1, f"xbuf{i}", [128, D], F32, "left") for i in range(2)]
    xn_tm = [sb(esP1, f"xn_tm{i}", [128, D], BF16, "left") for i in range(3)]
    lncol = sb(esP1, "lncol", [128, 16], F32, "left")
    xnstg = [sb(esP1, f"xnstg{i}", [128, 2048], BF16, "left")
             for i in range(2)]

    P.op("vector", lambda e: e.memset(v_sb[:], 1.0))
    P.op("vector", lambda e: e.memset(epscol[:], EPS))
    P.op("vector", lambda e: e.memset(onecol[:], 1.0))
    P.op("vector", lambda e: e.memset(ones_bf[:], 1.0))
    P.op("vector", lambda e: e.memset(ones_f[:], 1.0))
    P.op("vector", lambda e: e.drain(), "vS")
    consts_vS = P.cnt["vS"]

    pend1 = []

    def emit_p1_transposes():
        t0, par0, xn0 = pend1.pop(0)
        is_q = t0 >= 16
        tloc = t0 - 16 if is_q else t0
        if not is_q:
            P.acquire("vector", f"xnstg{t0 % 2}")
        for hb in range(2):
            P.wait("tensor", "vS", xn0)
            P.acquire("tensor", f"lnbank{par0 % 2}")
            for cc in range(8):
                c = hb * 8 + cc
                P.op("tensor", lambda e, par0=par0, c=c: e.transpose(
                    bankc(par0 % 2, (c % 8) * 64, 64).bitcast(BF16),
                    xn_tm[par0][:, c * 128:(c + 1) * 128], id_bf[:]),
                    "tS" if cc == 7 else None)
            t_b = P.cnt["tS"]
            P.wait("vector", "tS", t_b)
            for cc in range(8):
                c = hb * 8 + cc
                if is_q:
                    P.op("vector", lambda e, par0=par0, c=c, tl=tloc:
                         e.tensor_copy(
                             xn_own[:, c * 1024 + tl * 128:
                                    c * 1024 + (tl + 1) * 128],
                             bankc(par0 % 2, (c % 8) * 64, 64).bitcast(BF16)),
                         "vS" if cc == 7 else None)
                else:
                    P.op("vector", lambda e, par0=par0, c=c, t0=t0:
                         e.tensor_copy(
                             xnstg[t0 % 2][:, c * 128:(c + 1) * 128],
                             bankc(par0 % 2, (c % 8) * 64, 64).bitcast(BF16)),
                         "vS" if cc == 7 else None)
            P.release(f"lnbank{par0 % 2}", "vS")
        P.release(f"xn_tm{par0}", "tS", t_b)
        if not is_q:
            stg_vS = P.cnt["vS"]
            P.wait("sync", "vS", stg_vS)
            P.op("sync", lambda e, t0=t0: e.dma_start(
                out=xn_dram[t0].rearrange("p c t -> p (c t)"),
                in_=xnstg[t0 % 2][:]), f"dH{t0 % 2}", 16)
            P.release(f"xnstg{t0 % 2}", f"dH{t0 % 2}")
            xn_st[t0] = (f"dH{t0 % 2}", P.cnt[f"dH{t0 % 2}"])

    xsem = ["dX0", "dX1"]
    xn_st = {}
    for t in range(24):
        par = t % 2
        P.acquire("sync", f"xbuf{par}")
        P.op("sync", lambda e, t=t, par=par: e.dma_start(
            out=xbuf[par][:],
            in_=(x_in[t * 128:(t + 1) * 128, :] if t < 16 else
                 xq_in[(t - 16) * 128:(t - 15) * 128, :])),
            xsem[par], 16)
        x_dX = P.cnt[xsem[par]]

        mcol = lncol[:, par * 5: par * 5 + 1]
        sqcol = lncol[:, par * 5 + 1: par * 5 + 2]
        msq = lncol[:, par * 5 + 2: par * 5 + 3]
        vcol = lncol[:, par * 5 + 3: par * 5 + 4]
        rcol = lncol[:, par * 5 + 4: par * 5 + 5]
        P.wait("scalar", xsem[par], x_dX)
        P.acquire("scalar", f"sq{par}")
        P.op("scalar", lambda e: e.drain())
        P.op("scalar", lambda e, par=par, s=sqcol: e.activation(
            out=arena[:, 2048:4096], in_=xbuf[par][:], func=AF.Square,
            accum_out=s), "sS")
        sq_sS = P.cnt["sS"]
        P.wait("vector", xsem[par], x_dX)
        P.op("vector", lambda e: e.drain())
        P.op("vector", lambda e, par=par, m=mcol: e.reduce_sum(
            m, xbuf[par][:], axis=AX.X))
        P.op("vector", lambda e: e.drain())
        P.op("vector", lambda e, m=mcol: e.tensor_scalar_mul(m, m, 1.0 / D))
        P.op("vector", lambda e: e.drain())
        P.op("vector", lambda e, m=mcol, o=msq: e.tensor_mul(o, m, m))
        P.op("vector", lambda e: e.drain())
        P.wait("vector", "sS", sq_sS)
        P.op("vector", lambda e, s=sqcol, o=msq, v=vcol:
             e.scalar_tensor_tensor(out=v, in0=s, scalar=1.0 / D, in1=o,
                                    op0=ALU.mult, op1=ALU.subtract), "vS")
        var_vS = P.cnt["vS"]
        P.wait("scalar", "vS", var_vS)
        P.op("scalar", lambda e: e.drain())
        P.op("scalar", lambda e, v=vcol: e.activation(
            out=v, in_=v, func=AF.Ln, bias=epscol[:]))
        P.op("scalar", lambda e: e.drain())
        P.op("scalar", lambda e, v=vcol, r=rcol: e.activation(
            out=r, in_=v, func=AF.Exp, scale=-0.5), "sS")
        rstd_sS = P.cnt["sS"]
        P.release(f"sq{par}", "sS", rstd_sS)
        P.wait("vector", "sS", rstd_sS)
        P.acquire("vector", f"xn_tm{t % 3}")
        P.op("vector", lambda e: e.drain())
        P.op("vector", lambda e, par=par, t3=t % 3, m=mcol, r=rcol:
             e.tensor_scalar(
                 out=xn_tm[t3][:], in0=xbuf[par][:], scalar1=m, scalar2=r,
                 op0=ALU.subtract, op1=ALU.mult), "vS")
        xn_vS = P.cnt["vS"]
        P.release(f"xbuf{par}", "vS")
        if t == 0:
            P.wait("tensor", "dID", 16)
        pend1.append((t, t % 3, xn_vS))
        if len(pend1) > 2:
            emit_p1_transposes()
    while pend1:
        emit_p1_transposes()
    esP1.close()
    p1_vS = P.cnt["vS"]
    p1_tS = P.cnt["tS"]

    esR = ExitStack()
    kcos_sb = sb(esR, "kcos_sb", [128, 16 * 64], F32, "right")
    ksin_sb = sb(esR, "ksin_sb", [128, 16 * 64], F32, "right")
    qcos_sb = sb(esR, "qcos_sb", [128, 8 * 64], F32, "right")
    qsin_sb = sb(esR, "qsin_sb", [128, 8 * 64], F32, "right")
    rope_t1 = sb(esR, "rope_t1", [128, 512], F32, "right")
    rope_t2 = sb(esR, "rope_t2", [128, 512], F32, "right")
    kq_tm = [sb(esR, f"kq_tm{i}", [128, 512], BF16, "right") for i in range(2)]
    xnb = [sb(esR, f"xnb{i}", [128, 2048], BF16, "right") for i in range(2)]
    wbuf = [sb(esR, f"wbuf{i}", [128, 16 * 512], BF16, "right")
            for i in range(2)]

    for tab, src in [(kcos_sb, kcos_in), (ksin_sb, ksin_in),
                     (qcos_sb, qcos_in), (qsin_sb, qsin_in)]:
        nt = src.shape[0] // 128
        P.op("sync", lambda e, t=tab, s=src, nt=nt: e.dma_start(
            out=t[:].rearrange("p (t d) -> p t d", d=Dh),
            in_=s.rearrange("(t p) d -> p t d", p=128)), "dT", 16)
    tab_dT = P.cnt["dT"]

    xnld = {}

    def load_xn_tile(i, tc):
        P.acquire("sync", f"xnb{i % 2}")
        P.wait("sync", xn_st[tc][0], xn_st[tc][1])
        P.op("sync", lambda e, i=i, tc=tc: e.dma_start(
            out=xnb[i % 2][:], in_=xn_dram[tc].rearrange("p c t -> p (c t)")),
            f"dXN{i % 2}", 16)
        xnld[i] = P.cnt[f"dXN{i % 2}"]

    def rope_pass(psrc, tm_dst, cos_sb_, sin_sb_, tc):
        cosv = cos_sb_[:, tc * 64:(tc + 1) * 64].unsqueeze(1) \
            .broadcast_to([128, 8, 64])
        sinA = sin_sb_[:, tc * 64: tc * 64 + 32].unsqueeze(1) \
            .broadcast_to([128, 8, 32])
        sinB = sin_sb_[:, tc * 64 + 32: (tc + 1) * 64].unsqueeze(1) \
            .broadcast_to([128, 8, 32])
        r3 = lambda ap: ap.rearrange("p (h d) -> p h d", h=8)
        P.op("vector", lambda e: e.drain())
        P.op("vector", lambda e, ps=psrc, cv=cosv: e.tensor_tensor(
            out=r3(rope_t1[:]), in0=r3(ps), in1=cv, op=ALU.mult))
        P.op("vector", lambda e, ps=psrc, sv=sinA: e.tensor_tensor(
            out=r3(rope_t2[:])[:, :, 0:32], in0=r3(ps)[:, :, 32:64],
            in1=sv, op=ALU.mult))
        P.op("vector", lambda e, ps=psrc, sv=sinB: e.tensor_tensor(
            out=r3(rope_t2[:])[:, :, 32:64], in0=r3(ps)[:, :, 0:32],
            in1=sv, op=ALU.mult))
        P.op("vector", lambda e: e.drain())
        P.op("vector", lambda e, d=tm_dst: e.tensor_add(
            d, rope_t1[:], rope_t2[:]), "vS")

    for s in range(2):
        P.acquire("gpsimd", f"wbuf{s % 2}")
        P.op("gpsimd", lambda e, s=s: e.dma_start(
            out=wbuf[s % 2][:], in_=wk_t[s].rearrange("p c f -> p (c f)")),
            f"dWB{s % 2}", 16)
        P.release(f"wld{s % 2}", f"dWB{s % 2}")
    pendk = []

    def emit_kq_tr(dest, pitch):
        s0, tc0, par0, ktm0 = pendk.pop(0)
        P.wait("tensor", "vS", ktm0)
        P.acquire("tensor", f"trbank{par0}")
        for l in range(4):
            P.op("tensor", lambda e, p2=par0, l=l: e.transpose(
                bankc(p2, l * 64, 64).bitcast(BF16),
                kq_tm[p2][:, l * 128:(l + 1) * 128], id_bf[:]),
                "tS" if l == 3 else None)
        tr_tS = P.cnt["tS"]
        P.wait("vector", "tS", tr_tS)
        for l in range(4):
            slab = s0 * 4 + l
            P.op("vector", lambda e, p2=par0, l=l, slab=slab, tc0=tc0,
                 d=dest, pi=pitch: e.tensor_copy(
                     d[:, slab * pi + tc0 * 128:
                       slab * pi + (tc0 + 1) * 128],
                     bankc(p2, l * 64, 64).bitcast(BF16)),
                 "vS" if l == 3 else None)
        P.release(f"trbank{par0}", "vS")
        P.release(f"kq_tm{par0}", "vS")

    load_xn_tile(0, 0)
    load_xn_tile(1, 1)
    for s in range(2):
        for tc in range(NTC):
            it = s * NTC + tc
            pj = 2 + it % 2
            par2 = it % 2
            P.acquire("tensor", f"kvq_bank{pj}")
            P.wait("tensor", P.rel[f"wld{s % 2}"][0][0], P.rel[f"wld{s % 2}"][0][1])
            P.wait("tensor", f"dXN{it % 2}", xnld[it])
            for c in range(16):
                P.op("tensor", lambda e, c=c, pj=pj, it=it, s=s: e.matmul(
                    bank(pj), xnb[it % 2][:, c * 128:(c + 1) * 128],
                    wbuf[s % 2][:, c * 512:(c + 1) * 512],
                    start=(c == 0), stop=(c == 15)),
                    "tS" if c == 15 else None)
            mm_tS = P.cnt["tS"]
            P.release(f"xnb{it % 2}", "tS", mm_tS)
            if it + 2 < 64:
                load_xn_tile(it + 2, (it + 2) % NTC)
            if tc == NTC - 1:
                P.release(f"wbuf{s % 2}", "tS", mm_tS)
            P.wait("vector", "tS", mm_tS)
            P.wait("vector", "dT", tab_dT)
            P.acquire("vector", f"kq_tm{par2}")
            rope_pass(bank(pj), kq_tm[par2][:], kcos_sb, ksin_sb, tc)
            P.release(f"kvq_bank{pj}", "vS")
            pendk.append((s, tc, par2, P.cnt["vS"]))
            if len(pendk) > 1:
                emit_kq_tr(k_fm, 2048)
    while pendk:
        emit_kq_tr(k_fm, 2048)

    for s in range(2):
        P.acquire("gpsimd", f"wbuf{s % 2}")
        P.op("gpsimd", lambda e, s=s: e.dma_start(
            out=wbuf[s % 2][:], in_=wv_t[s].rearrange("p c f -> p (c f)")),
            f"dWB{s % 2}", 16)
        P.release(f"wld{s % 2}", f"dWB{s % 2}")
    for s in range(2):
        for tc in range(NTC):
            it = 32 + s * NTC + tc
            pj = 2 + it % 2
            P.acquire("tensor", f"kvq_bank{pj}")
            P.wait("tensor", P.rel[f"wld{s % 2}"][0][0], P.rel[f"wld{s % 2}"][0][1])
            P.wait("tensor", f"dXN{it % 2}", xnld[it])
            if it == 32:
                P.wait("tensor", "vS", consts_vS)
            for c in range(16):
                P.op("tensor", lambda e, c=c, pj=pj, it=it, s=s: e.matmul(
                    bank(pj), xnb[it % 2][:, c * 128:(c + 1) * 128],
                    wbuf[s % 2][:, c * 512:(c + 1) * 512],
                    start=(c == 0), stop=(c == 15)),
                    "tS" if c == 15 else None)
            mm_tS = P.cnt["tS"]
            P.release(f"xnb{it % 2}", "tS", mm_tS)
            if it + 2 < 64:
                load_xn_tile(it + 2, (it + 2) % NTC)
            if tc == NTC - 1:
                P.release(f"wbuf{s % 2}", "tS", mm_tS)
            P.wait("vector", "tS", mm_tS)
            for l in range(4):
                j2 = s * 4 + l
                P.op("vector", lambda e, pj=pj, l=l, j2=j2, tc=tc:
                     e.tensor_copy(
                         v_sb[:, (j2 * 16 + tc) * 132:
                              (j2 * 16 + tc) * 132 + 128],
                         bankc(pj, l * 128, 128)),
                     "vS" if l == 3 else None)
            P.release(f"kvq_bank{pj}", "vS")

    q_fm = sb(esKVQ, "q_fm", [128, 16 * 1024], BF16, "left")

    def load_q_stripe(s):
        P.acquire("gpsimd", f"wbuf{s % 2}")
        P.op("gpsimd", lambda e, s=s: e.dma_start(
            out=wbuf[s % 2][:], in_=wq_t[s].rearrange("p c f -> p (c f)")),
            f"dWB{s % 2}", 16)
        P.release(f"wld{s % 2}", f"dWB{s % 2}")

    load_q_stripe(0)
    load_q_stripe(1)
    for s in range(4):
        for tc in range(NTO):
            it = s * NTO + tc
            pj = 2 + it % 2
            par2 = it % 2
            P.acquire("tensor", f"kvq_bank{pj}")
            P.wait("tensor", P.rel[f"wld{s % 2}"][0][0], P.rel[f"wld{s % 2}"][0][1])
            for c in range(16):
                P.op("tensor", lambda e, tc=tc, c=c, pj=pj, s=s: e.matmul(
                    bank(pj),
                    xn_own[:, c * 1024 + tc * 128: c * 1024 + (tc + 1) * 128],
                    wbuf[s % 2][:, c * 512:(c + 1) * 512],
                    start=(c == 0), stop=(c == 15)),
                    "tS" if c == 15 else None)
            mm_tS = P.cnt["tS"]
            if tc == NTO - 1:
                P.release(f"wbuf{s % 2}", "tS", mm_tS)
                if s + 2 < 4:
                    load_q_stripe(s + 2)
            P.wait("vector", "tS", mm_tS)
            P.acquire("vector", f"kq_tm{par2}")
            rope_pass(bank(pj), kq_tm[par2][:], qcos_sb, qsin_sb, tc)
            P.release(f"kvq_bank{pj}", "vS")
            pendk.append((s, tc, par2, P.cnt["vS"]))
            if len(pendk) > 1:
                emit_kq_tr(q_fm, 1024)
    while pendk:
        emit_kq_tr(q_fm, 1024)
    esR.close()
    kvq_sS = P.cnt["sS"]
    kvq_vS = P.cnt["vS"]
    kvq_tS = P.cnt["tS"]

    P.wait("vector", "dT", tab_dT)
    P.op("vector", lambda e: e.drain())
    P.op("vector", lambda e: e.tensor_mul(
        small[:, 384:448], small[:, 0:64], small[:, 64:128]))
    P.op("vector", lambda e: e.tensor_mul(
        small[:, 448:512], small[:, 128:192], small[:, 192:256]))
    P.op("vector", lambda e: e.drain())
    P.op("vector", lambda e: e.reduce_sum(
        small[:, 512:514].rearrange("p (a b) -> p a b", a=2),
        small[:, 384:512].rearrange("p (a b) -> p a b", a=2), axis=AX.X))
    P.op("vector", lambda e: e.drain(), "vS")
    lam_vS = P.cnt["vS"]
    P.wait("scalar", "vS", lam_vS)
    P.op("scalar", lambda e: e.drain())
    P.op("scalar", lambda e: e.activation(
        out=small[:, 514:516], in_=small[:, 512:514], func=AF.Exp), "sS")
    lam_sS = P.cnt["sS"]
    P.wait("vector", "sS", lam_sS)
    P.op("vector", lambda e: e.drain())
    P.op("vector", lambda e: e.tensor_sub(
        small[:, 516:517], small[:, 515:516], small[:, 514:515]))
    P.op("vector", lambda e: e.drain())
    P.op("vector", lambda e: e.tensor_scalar_add(
        small[:, 516:517], small[:, 516:517], -LAMBDA_INIT), "vS")

    lamn_vS = P.cnt["vS"]
    P.wait("tensor", "sS", kvq_sS)
    P.wait("tensor", "vS", kvq_vS)
    P.wait("tensor", "dT", tab_dT)
    P.acquire("tensor", "trbank0")
    P.acquire("tensor", "trbank1")
    P.acquire("tensor", "lnbank0")
    P.acquire("tensor", "lnbank1")
    P.wait("tensor", "vS", lamn_vS)
    P.op("tensor", lambda e: e.matmul(
        bankc(0, 64, 1), ones_f[:], small[:, 516:517], start=True, stop=True))
    P.op("tensor", lambda e: e.matmul(
        bankc(1, 0, 128), ones_f[:], small[:, 256:384], start=True,
        stop=True), "tS")
    bc_tS = P.cnt["tS"]
    P.wait("vector", "tS", bc_tS)
    P.op("vector", lambda e: e.drain())
    P.op("vector", lambda e: e.tensor_copy(lamn_col[:], bankc(0, 64, 1)))
    P.op("vector", lambda e: e.tensor_scalar_mul(
        w_bcast[:], bankc(1, 0, 128), 1.0 - LAMBDA_INIT))
    P.op("vector", lambda e: e.drain(), "vS")
    bcv_vS = P.cnt["vS"]

    es5 = ExitStack()
    qa_sb = sb(es5, "qa_sb", [128, 1024], BF16, "right")
    tb2 = sb(es5, "tb2", [32, 128], BF16, "right")
    P.wait("vector", "tS", kvq_tS)
    P.op("vector", lambda e: e.drain())
    for j2 in range(8):
        P.op("vector", lambda e, j2=j2: e.reduce_max(
            kdmax_f[:, j2:j2 + 1], k_fm[:, j2 * 2048:(j2 + 1) * 2048],
            axis=AX.X, apply_absolute_value=True))
    P.op("vector", lambda e: e.drain())
    P.op("vector", lambda e: e.tensor_copy(kdmax_b[:], kdmax_f[:]))
    P.op("vector", lambda e: e.drain(), "vS")
    kd_vS = P.cnt["vS"]
    P.acquire("tensor", "kvq_bank2")
    P.acquire("tensor", "kvq_bank3")
    for slab in range(16):
        j2 = slab // 2
        P.acquire("scalar", "qa_sb")
        if slab == 0:
            P.wait("scalar", "vS", kvq_vS)
        P.op("scalar", lambda e: e.drain())
        for tc in range(NTO):
            P.op("scalar", lambda e, slab=slab, tc=tc: e.activation(
                out=qa_sb[:, tc * 128:(tc + 1) * 128],
                in_=q_fm[:, slab * 1024 + tc * 128:
                         slab * 1024 + (tc + 1) * 128],
                func=AF.Abs),
                "sS" if tc == NTO - 1 else None)
        qa_sS = P.cnt["sS"]
        P.wait("tensor", "sS", qa_sS)
        P.wait("tensor", "vS", kd_vS)
        for tc in range(NTO):
            for h in range(2):
                idx = tc * 32 + slab * 2 + h
                P.op("tensor",
                     lambda e, slab=slab, tc=tc, h=h, j2=j2, idx=idx:
                     e.matmul(
                         bankc(2 + idx // 128, idx % 128, 1),
                         qa_sb[h * 64:(h + 1) * 64,
                               tc * 128:(tc + 1) * 128],
                         kdmax_b[h * 64:(h + 1) * 64, j2:j2 + 1],
                         start=True, stop=True),
                     "tS" if (tc == NTO - 1 and h == 1) else None)
        bm_tS = P.cnt["tS"]
        P.release("qa_sb", "tS", bm_tS)
    P.wait("vector", "tS", P.cnt["tS"])
    P.op("vector", lambda e: e.drain())
    P.op("vector", lambda e: e.tensor_scalar_mul(
        b_sb[:, 0:128], bankc(2, 0, 128), -1.0))
    P.op("vector", lambda e: e.tensor_scalar_mul(
        b_sb[:, 128:256], bankc(3, 0, 128), -1.0), "vS")
    P.release("kvq_bank2", "vS")
    P.release("kvq_bank3", "vS")
    bsb_vS = P.cnt["vS"]
    for qb in range(8):
        P.wait("tensor", "vS", bsb_vS)
        P.acquire("tensor", "lnbank0")
        P.op("tensor", lambda e, qb=qb: e.transpose(
            arena[0:32, 0:128], b_sb[:, qb * 32:(qb + 1) * 32], id_f[:]),
            "tS")
        ntr_tS = P.cnt["tS"]
        P.wait("vector", "tS", ntr_tS)
        P.acquire("vector", "tb2")
        P.op("vector", lambda e: e.drain())
        P.op("vector", lambda e: e.tensor_copy(tb2[:], arena[0:32, 0:128]),
             "vS")
        P.release("lnbank0", "vS")
        ntb_vS = P.cnt["vS"]
        P.wait("sync", "vS", ntb_vS)
        P.op("sync", lambda e, qb=qb: e.dma_start(
            out=negb_dram[qb], in_=tb2[:]), "dST0", 16)
        P.release("tb2", "dST0")
    negb_st = P.cnt["dST0"]
    es5.close()
    p5_vS = P.cnt["vS"]
    p5_sS = P.cnt["sS"]

    esA1 = ExitStack()
    e_sb = [sb(esA1, f"e_sb{g}", [128, 2048], BF16, "right") for g in range(4)]
    mask_b = sb(esA1, "mask_b", [128, 2048], BF16, "right")
    negb_b = sb(esA1, "negb_b", [128, 4096], BF16, "right")
    cmb = sb(esA1, "cmb", [128, 16], F32, "right")
    tsc_sb = sb(esA1, "tsc_sb", [128, 128], F32, "right")
    attnf_sb = [sb(esA1, f"attnf{m}", [128, 128], F32, "right")
                for m in range(2)]
    attnb_sb = [[sb(esA1, f"attnb{s}_{m}", [128, 128], BF16, "right")
                 for m in range(2)] for s in range(4)]
    sqd2 = sb(esA1, "sqd2", [128, 128], F32, "right")
    u1_sb = sb(esA1, "u1_sb", [128, 264], F32, "right")
    wg_b = [sb(esA1, f"wg_b{i}", [128, 2048], BF16, "right") for i in range(3)]
    wu_b = [sb(esA1, f"wu_b{i}", [128, 2048], BF16, "right") for i in range(3)]
    sg_sb = [sb(esA1, f"sg_sb{i}", [128, 512], F32, "right") for i in range(2)]

    P.op("vector", lambda e: e.memset(negb_b[:], 0.0))
    P.op("vector", lambda e: e.drain(), "vS")
    nbz_vS = P.cnt["vS"]

    ffn = {"i": 0}

    def emit_ffn_chunk():
        i = ffn["i"]
        if i >= 256:
            return
        ffn["i"] = i + 1
        if i % 12 == 0 and i // 12 < 20:
            ic = i // 12
            if ic < 16:
                P.op("gpsimd", lambda e, ic=ic: e.dma_start(
                    out=wd_bf[ic // 4, ic % 4].rearrange("p c f -> p (c f)"),
                    in_=wd_t[ic // 4, ic % 4].rearrange("p c f -> p (c f)")),
                    "dWC", 16)
            else:
                P.op("gpsimd", lambda e, ic=ic: e.dma_start(
                    out=wo_bf[ic - 16].rearrange("p c f -> p (c f)"),
                    in_=wo_t[ic - 16].rearrange("p c f -> p (c f)")),
                    "dWC", 16)
        ig, sub = i // 4, i % 4
        par = ig % 3
        half = sub // 2
        hp = (ig * 2 + half) % 2
        if sub == 0:
            P.acquire("gpsimd", f"wg_b{par}")
            if ig == 0:
                P.wait("gpsimd", "vS", p5_vS)
            P.op("gpsimd", lambda e, ig=ig, par=par: e.dma_start(
                out=wg_b[par][:],
                in_=wgu_t[ig].rearrange("p c f -> p (c f)")), f"dGU{par}", 16)
            P.acquire("gpsimd", f"wu_b{par}")
            P.op("gpsimd", lambda e, ig=ig, par=par: e.dma_start(
                out=wu_b[par][:],
                in_=wgu_t[64 + ig].rearrange("p c f -> p (c f)")),
                f"dGU{par}", 16)
            ffn["dGU"] = (f"dGU{par}", P.cnt[f"dGU{par}"])
        if sub % 2 == 0:
            P.wait("tensor", ffn["dGU"][0], ffn["dGU"][1])
            P.acquire("tensor", "ffnb6")
            for c in range(16):
                P.op("tensor", lambda e, c=c, half=half, par=par:
                     e.matmul(bankc(6, 0, 512),
                              wg_b[par][:, c * 128:(c + 1) * 128],
                              xn_own[:, c * 1024 + half * 512:
                                     c * 1024 + half * 512 + 512],
                              start=(c == 0), stop=(c == 15)),
                     "tS" if c == 15 else None)
            ffn["g_tS"] = P.cnt["tS"]
        else:
            P.acquire("tensor", "ffnb7")
            for c in range(16):
                P.op("tensor", lambda e, c=c, half=half, par=par:
                     e.matmul(bankc(7, 0, 512),
                              wu_b[par][:, c * 128:(c + 1) * 128],
                              xn_own[:, c * 1024 + half * 512:
                                     c * 1024 + half * 512 + 512],
                              start=(c == 0), stop=(c == 15)),
                     "tS" if c == 15 else None)
            gu_tS = P.cnt["tS"]
            if sub == 3:
                P.release(f"wg_b{par}", "tS", gu_tS)
                P.release(f"wu_b{par}", "tS", gu_tS)
            P.wait("scalar", "tS", gu_tS)
            P.acquire("scalar", f"sg_sb{hp}")
            P.op("scalar", lambda e: e.drain())
            P.op("scalar", lambda e, hp=hp: e.activation(
                out=sg_sb[hp][:], in_=bankc(6, 0, 512), func=AF.Exp,
                scale=-1.0))
            P.op("scalar", lambda e: e.drain())
            P.op("scalar", lambda e, hp=hp: e.activation(
                out=sg_sb[hp][:], in_=sg_sb[hp][:], func=AF.Ln,
                bias=onecol[:]))
            P.op("scalar", lambda e: e.drain())
            P.op("scalar", lambda e, hp=hp: e.activation(
                out=sg_sb[hp][:], in_=sg_sb[hp][:], func=AF.Exp,
                scale=-1.0), "sS")
            sg_sS = P.cnt["sS"]
            P.wait("vector", "sS", sg_sS)
            P.acquire("vector", f"h_stg{hp}")
            P.op("vector", lambda e: e.drain())
            P.op("vector", lambda e, hp=hp: e.scalar_tensor_tensor(
                out=sg_sb[hp][:], in0=sg_sb[hp][:], scalar=1.0,
                in1=bankc(6, 0, 512), op0=ALU.mult, op1=ALU.mult), "vS")
            sm_vS = P.cnt["vS"]
            P.release("ffnb6", "vS", sm_vS)
            P.op("vector", lambda e: e.drain())
            P.op("vector", lambda e, hp=hp: e.scalar_tensor_tensor(
                out=h_stg[hp][:], in0=sg_sb[hp][:], scalar=1.0,
                in1=bankc(7, 0, 512), op0=ALU.mult, op1=ALU.mult), "vS")
            hm_vS = P.cnt["vS"]
            P.release("ffnb7", "vS", hm_vS)
            P.release(f"sg_sb{hp}", "vS", hm_vS)
            P.wait("sync", "vS", hm_vS)
            P.op("sync", lambda e, ig=ig, half=half, hp=hp: e.dma_start(
                out=h_dram[ig, :, half * 512:half * 512 + 512],
                in_=h_stg[hp][:]), f"dH{hp}", 16)
            P.release(f"h_stg{hp}", f"dH{hp}")

    pend = []
    pend_av = []
    LAG = 2
    LIM = [5, 6, 7, 8, 13, 14, 15, 16]

    def emit_avA(j, LQ):
        kq0, nk0, mvS = pend_av.pop(0)
        P.wait("tensor", "vS", mvS)
        if kq0 == 0:
            P.acquire("tensor", "ubankA")
        for kk in range(nk0):
            kb = kq0 * 4 + kk
            for g in range(2):
                P.op("tensor", lambda e, g=g, kb=kb, j=j, LQ=LQ: e.matmul(
                    bankc(4 + g, 0, 129),
                    e_sb[g][:, kb * 128:(kb + 1) * 128],
                    v_sb[:, (j * 16 + kb) * 132:
                         (j * 16 + kb) * 132 + 129],
                    start=(kb == 0), stop=(kb == LQ - 1)),
                    "tS" if (kb == LQ - 1 and g == 1) else None)
    TOTKB = 8 * sum(LIM)
    kbctr = {"n": 0}

    def paced_ffn_chunk():
        if ffn["i"] * TOTKB < 256 * kbctr["n"]:
            emit_ffn_chunk()

    def emit_attn_store():
        qb0, j0, nrm0 = pend.pop(0)
        blk = qb0 * 8 + j0
        slot = blk % 4
        blkpar = blk % 2
        P.wait("sync", "vS", nrm0)
        for m in range(2):
            P.op("sync", lambda e, m=m, slot=slot, j0=j0, qb0=qb0:
                 e.dma_start(
                     out=attn_dram[qb0 * 128:(qb0 + 1) * 128,
                                   (2 * j0 + m) * 128:(2 * j0 + m + 1) * 128],
                     in_=attnb_sb[slot][m][:]), f"dAT{slot}", 16)
        P.release(f"attnb{slot}_0", f"dAT{slot}")
        P.release(f"attnb{slot}_1", f"dAT{slot}")

    P.wait("scalar", "vS", p5_vS)
    P.wait("sync", "vS", p5_vS)
    P.wait("sync", "dWB0", P.cnt["dWB0"])
    P.wait("sync", "dWB1", P.cnt["dWB1"])
    P.wait("sync", "dX0", P.cnt["dX0"])
    P.wait("sync", "dX1", P.cnt["dX1"])
    P.wait("sync", "vS", nbz_vS)
    P.wait("sync", "dST0", negb_st)
    for qb in range(8):
        P.acquire("sync", "mask_b")
        P.op("sync", lambda e, qb=qb: e.dma_start(
            out=mask_b[:],
            in_=mask_in[:, qb * 2048:(qb + 1) * 2048]), "dM0", 16)
        P.acquire("sync", "negb_b")
        P.op("sync", lambda e, qb=qb: e.dma_start(
            out=negb_b[0:1, :].rearrange("p (s c) -> p s c", s=32),
            in_=negb_dram[qb].unsqueeze(0)), "dM0", 16)
        P.op("sync", lambda e, qb=qb: e.dma_start(
            out=negb_b[64:65, :].rearrange("p (s c) -> p s c", s=32),
            in_=negb_dram[qb].unsqueeze(0)), "dM0", 16)
        mask_dM = ("dM0", P.cnt["dM0"])
        for j in range(8):
            LQ = LIM[qb]
            kbctr["n"] += LQ
            for kq in range((LQ + 3) // 4):
                nk = min(4, LQ - 4 * kq)
                P.acquire("tensor", "stripes")
                if qb == 0 and j == 0 and kq == 0:
                    P.wait("tensor", "vS", p5_vS)
                    P.wait("tensor", "sS", p5_sS)
                if j == 0 and kq == 0:
                    P.wait("tensor", mask_dM[0], mask_dM[1])
                for g in range(4):
                    rows = (g % 2) * 64
                    slab = 2 * j + (1 if g >= 2 else 0)
                    qhh = slab * 2 + g % 2
                    for kk in range(nk):
                        kb = kq * 4 + kk
                        P.op("tensor",
                             lambda e, g=g, rows=rows, slab=slab, kb=kb,
                             kk=kk, j=j, qb=qb: e.matmul(
                                 bankc(g, (kb % 4) * 128, 128),
                                 k_fm[rows:rows + 64,
                                      j * 2048 + kb * 128:
                                      j * 2048 + (kb + 1) * 128],
                                 q_fm[rows:rows + 64,
                                      slab * 1024 + qb * 128:
                                      slab * 1024 + (qb + 1) * 128],
                                 start=(kk == 0), stop=False))
                    P.op("tensor",
                         lambda e, g=g, rows=rows, qhh=qhh, nk=nk: e.matmul(
                             bankc(g, 0, nk * 128),
                             ones_bf[rows:rows + 64, :],
                             negb_b[rows:rows + 64,
                                    qhh * 128:(qhh + 1) * 128]
                             .unsqueeze(1).broadcast_to([64, nk, 128]),
                             start=False, stop=True),
                         "tS" if g == 3 else None)
                sc_tS = P.cnt["tS"]
                P.wait("scalar", "tS", sc_tS)
                for g in range(4):
                    P.op("scalar", lambda e, g=g, kq=kq, nk=nk:
                         e.activation(
                             out=e_sb[g][:, kq * 512:kq * 512 + nk * 128],
                             in_=bankc(g, 0, nk * 128), func=AF.Exp,
                             scale=1.0),
                         "sS")
                P.release("stripes", "sS")
                exp_sS = P.cnt["sS"]
                P.wait("vector", "sS", exp_sS)
                if j == 0 and kq == 0:
                    P.wait("vector", mask_dM[0], mask_dM[1])
                P.op("vector", lambda e: e.drain())
                for g in range(4):
                    P.op("vector", lambda e, g=g, kq=kq, nk=nk: e.tensor_mul(
                        e_sb[g][:, kq * 512:kq * 512 + nk * 128],
                        e_sb[g][:, kq * 512:kq * 512 + nk * 128],
                        mask_b[:, kq * 512:kq * 512 + nk * 128]),
                        "vS" if g == 3 else None)
                pend_av.append((kq, nk, P.cnt["vS"]))
                paced_ffn_chunk()
                if len(pend_av) > 1:
                    emit_avA(j, LQ)
            while pend_av:
                emit_avA(j, LQ)
            mask_vS = P.cnt["vS"]
            if j == 7:
                P.release("mask_b", "vS", mask_vS)
                P.release("negb_b", "tS")
            avA_tS = P.cnt["tS"]
            P.wait("vector", "tS", avA_tS)
            P.op("vector", lambda e: e.drain())
            P.op("vector", lambda e: e.tensor_copy(
                u1_sb[:, 0:129], bankc(4, 0, 129)))
            P.op("vector", lambda e: e.tensor_copy(
                u1_sb[:, 132:261], bankc(5, 0, 129)), "vS")
            u1_vS = P.cnt["vS"]
            P.wait("tensor", "vS", u1_vS)
            for kb in range(LQ):
                for g in range(2):
                    P.op("tensor", lambda e, g=g, kb=kb, j=j, LQ=LQ: e.matmul(
                        bankc(4 + g, 0, 129),
                        e_sb[2 + g][:, kb * 128:(kb + 1) * 128],
                        v_sb[:, (j * 16 + kb) * 132:
                             (j * 16 + kb) * 132 + 129],
                        start=(kb == 0), stop=(kb == LQ - 1)),
                        "tS" if (kb == LQ - 1 and g == 1) else None)
            avB_tS = P.cnt["tS"]
            paced_ffn_chunk()
            P.wait("vector", "tS", avB_tS)
            P.op("vector", lambda e: e.drain())
            P.op("vector", lambda e: e.tensor_copy(
                cmb[:, 0:1], u1_sb[:, 128:129]))
            P.op("vector", lambda e: e.tensor_copy(
                cmb[:, 1:2], u1_sb[:, 260:261]))
            P.op("vector", lambda e: e.tensor_copy(
                cmb[:, 2:3], bankc(4, 128, 1)))
            P.op("vector", lambda e: e.tensor_copy(
                cmb[:, 3:4], bankc(5, 128, 1)))
            P.op("vector", lambda e: e.drain())
            P.op("vector", lambda e: e.tensor_scalar_add(
                cmb[:, 0:4], cmb[:, 0:4], 1e-38))
            P.op("vector", lambda e: e.drain())
            P.op("vector", lambda e: e.reciprocal(cmb[:, 4:8], cmb[:, 0:4]))
            P.op("vector", lambda e: e.drain())
            P.wait("vector", "vS", lamn_vS)
            P.op("vector", lambda e: e.tensor_scalar_mul(
                cmb[:, 6:8], cmb[:, 6:8], lamn_col[:]))
            P.op("vector", lambda e: e.drain())
            for m in range(2):
                P.acquire("vector", f"attnf{m}")
                P.op("vector", lambda e, m=m: e.tensor_scalar_mul(
                    tsc_sb[:], u1_sb[:, m * 132:m * 132 + 128],
                    cmb[:, 4 + m: 5 + m]))
                P.op("vector", lambda e: e.drain())
                P.op("vector", lambda e, m=m: e.scalar_tensor_tensor(
                    out=attnf_sb[m][:], in0=bankc(4 + m, 0, 128),
                    scalar=cmb[:, 6 + m: 7 + m], in1=tsc_sb[:],
                    op0=ALU.mult, op1=ALU.add), "vS")
                P.op("vector", lambda e: e.drain())
            cmb_vS = P.cnt["vS"]
            P.wait("scalar", "vS", cmb_vS)
            P.op("scalar", lambda e: e.drain())
            for m in range(2):
                P.op("scalar", lambda e, m=m: e.activation(
                    out=sqd2[:], in_=attnf_sb[m][:], func=AF.Square,
                    accum_out=cmb[:, 8 + m: 9 + m]))
                P.op("scalar", lambda e: e.drain())
            P.op("scalar", lambda e: e.activation(
                out=cmb[:, 10:12], in_=cmb[:, 8:10], func=AF.Ln,
                scale=1.0 / 128, bias=epscol[:]))
            P.op("scalar", lambda e: e.drain())
            P.op("scalar", lambda e: e.activation(
                out=cmb[:, 12:14], in_=cmb[:, 10:12], func=AF.Exp,
                scale=-0.5), "sS")
            rstd_sS2 = P.cnt["sS"]
            P.wait("vector", "sS", rstd_sS2)
            P.op("vector", lambda e: e.drain())
            slot = (qb * 8 + j) % 4
            for m in range(2):
                P.acquire("vector", f"attnb{slot}_{m}")
                P.op("vector", lambda e, m=m, slot=slot:
                     e.scalar_tensor_tensor(
                         out=attnb_sb[slot][m][:], in0=attnf_sb[m][:],
                         scalar=cmb[:, 12 + m: 13 + m], in1=w_bcast[:],
                         op0=ALU.mult, op1=ALU.mult), "vS")
                P.release(f"attnf{m}", "vS")
            nrm_vS = P.cnt["vS"]
            P.release("ubankA", "vS", cmb_vS)
            pend.append((qb, j, nrm_vS))
            if len(pend) > LAG:
                emit_attn_store()
    while pend:
        emit_attn_store()
    while ffn["i"] < 256:
        emit_ffn_chunk()
    p6_vS = P.cnt["vS"]
    p6_sS = P.cnt["sS"]
    p6_tS = P.cnt["tS"]
    esA1.close()
    esKVQ.close()


    esO = ExitStack()
    wo_b = [sb(esO, f"wo_b{i}", [128, 16 * 512], BF16, "right")
            for i in range(2)]
    attn_b = sb(esO, "attn_b", [128, 16 * 1024], BF16, "right")
    P.wait("gpsimd", "tS", p6_tS)
    P.wait("gpsimd", "vS", p6_vS)
    for s4 in range(4):
        P.wait("gpsimd", f"dAT{s4}", P.cnt[f"dAT{s4}"])
    for s4 in range(4):
        P.wait("sync", f"dAT{s4}", P.cnt[f"dAT{s4}"])
    P.wait("sync", "vS", p6_vS)

    P.wait("gpsimd", "dWC", P.cnt["dWC"])

    def load_o_stripe(dc):
        P.acquire("gpsimd", f"wo_b{dc % 2}")
        P.op("gpsimd", lambda e, dc=dc: e.dma_start(
            out=wo_b[dc % 2][:], in_=wo_bf[dc].rearrange("p c f -> p (c f)")),
            f"dWB{dc % 2}", 16)
        P.release(f"wodma{dc % 2}", f"dWB{dc % 2}")

    load_o_stripe(0)
    load_o_stripe(1)
    for c in range(16):
        P.op("sync", lambda e, c=c: e.dma_start(
            out=attn_b[:, c * 1024:(c + 1) * 1024],
            in_=attn_dram[:, c * 128:(c + 1) * 128], transpose=True),
            "dX0", 16)
    at_dX = P.cnt["dX0"]
    for dc in range(4):
        for tc in range(NTO):
            it = dc * NTO + tc
            pj = it % 2
            P.acquire("tensor", f"kvq_bank{pj}")
            P.wait("tensor", P.rel[f"wodma{dc % 2}"][0][0],
                   P.rel[f"wodma{dc % 2}"][0][1])
            P.wait("tensor", "dX0", at_dX)
            if it == 0:
                P.wait("tensor", "vS", p6_vS)
                P.wait("tensor", "sS", p6_sS)
            for c in range(16):
                P.op("tensor", lambda e, tc=tc, c=c, pj=pj, dc=dc: e.matmul(
                    bank(pj),
                    attn_b[:, c * 1024 + tc * 128: c * 1024 + (tc + 1) * 128],
                    wo_b[dc % 2][:, c * 512:(c + 1) * 512],
                    start=(c == 0), stop=(c == 15)),
                    "tS" if c == 15 else None)
            mm_tS = P.cnt["tS"]
            if tc == NTO - 1:
                P.release(f"wo_b{dc % 2}", "tS", mm_tS)
                if dc + 2 < 4:
                    load_o_stripe(dc + 2)
            P.wait("vector", "tS", mm_tS)
            P.acquire("vector", f"stage{it % 2}")
            P.op("vector", lambda e, pj=pj, it=it: e.tensor_copy(
                stage[it % 2][:], bank(pj)), "vS")
            P.release(f"kvq_bank{pj}", "vS")
            st_vS = P.cnt["vS"]
            P.wait("sync", "vS", st_vS)
            P.op("sync", lambda e, it=it, tc=tc, dc=dc: e.dma_start(
                out=s_dram[tc * 128:(tc + 1) * 128, dc * 512:(dc + 1) * 512],
                in_=stage[it % 2][:]), f"dST{it % 2}", 16)
            P.release(f"stage{it % 2}", f"dST{it % 2}")
    esO.close()
    p7_vS = P.cnt["vS"]
    p7_tS = P.cnt["tS"]
    s_st0 = P.cnt["dST0"]
    s_st1 = P.cnt["dST1"]

    esD = ExitStack()
    wd_b = [sb(esD, f"wd_b{i}", [128, 16 * 512], BF16, "right")
            for i in range(2)]
    h_b = [sb(esD, f"h_b{i}", [128, 16 * 1024], BF16, "right")
           for i in range(2)]
    g_sb = [sb(esD, f"g_sb{i}", [128, 512], F32, "right") for i in range(2)]
    sbk = [sb(esD, f"sbk{i}", [128, 512], F32, "right") for i in range(2)]
    xbk = [sb(esD, f"xbk{i}", [128, 512], F32, "right") for i in range(2)]
    stg2 = [sb(esD, f"stg2{i}", [128, 512], F32, "right") for i in range(2)]

    P.wait("gpsimd", "vS", p7_vS)
    P.wait("gpsimd", "tS", p7_tS)
    P.wait("sync", "vS", p7_vS)
    P.wait("sync", "dWB0", P.cnt["dWB0"])
    P.wait("sync", "dWB1", P.cnt["dWB1"])
    P.wait("sync", "dST0", s_st0)
    P.wait("sync", "dST1", s_st1)
    P.wait("sync", "dH0", P.cnt["dH0"])
    P.wait("sync", "dH1", P.cnt["dH1"])
    wdh_sem = {}

    def load_wdh(iq):
        dc_, q_ = iq // 4, iq % 4
        P.acquire("gpsimd", f"wd_b{iq % 2}")
        P.op("gpsimd", lambda e, dc_=dc_, q_=q_, iq=iq: e.dma_start(
            out=wd_b[iq % 2][:],
            in_=wd_bf[dc_, q_].rearrange("p c f -> p (c f)")),
            f"dWD{iq % 2}", 16)
        P.acquire("sync", f"h_b{iq % 2}")
        P.op("sync", lambda e, q_=q_, iq=iq: e.dma_start(
            out=h_b[iq % 2][:].rearrange("p (c t) -> p c t", c=16),
            in_=h_dram[q_ * 16:(q_ + 1) * 16].rearrange("c p t -> p c t")),
            f"dSB{iq % 2}", 16)
        wdh_sem[iq] = (P.cnt[f"dWD{iq % 2}"], P.cnt[f"dSB{iq % 2}"])

    P.wait("gpsimd", "dWC", P.cnt["dWC"])
    load_wdh(0)
    load_wdh(1)
    for dc in range(4):
        for q in range(4):
            iq = dc * 4 + q
            P.wait("tensor", f"dWD{iq % 2}", wdh_sem[iq][0])
            P.wait("tensor", f"dSB{iq % 2}", wdh_sem[iq][1])
            if iq == 0:
                P.wait("tensor", "vS", p7_vS)
                P.wait("tensor", "sS", p6_sS)
            for tc in range(NTO):
                if q == 0:
                    P.acquire("tensor", f"dbank{tc}")
                for c in range(16):
                    P.op("tensor", lambda e, q=q, tc=tc, c=c, iq=iq: e.matmul(
                        bankc(tc, 0, 512),
                        h_b[iq % 2][:, c * 1024 + tc * 128:
                                    c * 1024 + (tc + 1) * 128],
                        wd_b[iq % 2][:, c * 512:(c + 1) * 512],
                        start=(q == 0 and c == 0),
                        stop=(q == 3 and c == 15)),
                        "tS" if (tc == NTO - 1 and c == 15) else None)
            P.release(f"wd_b{iq % 2}", "tS")
            P.release(f"h_b{iq % 2}", "tS")
            if iq + 2 < 16:
                load_wdh(iq + 2)
        dn_tS = P.cnt["tS"]
        for tc in range(NTO):
            it = dc * NTO + tc
            par = it % 2
            P.acquire("sync", f"sbk{par}")
            P.acquire("sync", f"xbk{par}")
            P.op("sync", lambda e, dc=dc, tc=tc, par=par: e.dma_start(
                out=sbk[par][:], in_=s_dram[tc * 128:(tc + 1) * 128,
                                           dc * 512:(dc + 1) * 512]),
                f"dM{par}", 16)
            P.op("sync", lambda e, dc=dc, tc=tc, par=par: e.dma_start(
                out=xbk[par][:], in_=xq_in[tc * 128:(tc + 1) * 128,
                                           dc * 512:(dc + 1) * 512]),
                f"dM{par}", 16)
            sx_dX = (f"dM{par}", P.cnt[f"dM{par}"])
            P.wait("scalar", "tS", dn_tS)
            P.acquire("scalar", f"g_sb{par}")
            P.op("scalar", lambda e: e.drain())
            P.op("scalar", lambda e, tc=tc, par=par: e.activation(
                out=g_sb[par][:], in_=bankc(tc, 0, 512), func=AF.Exp,
                scale=-1.0))
            P.op("scalar", lambda e: e.drain())
            P.op("scalar", lambda e, par=par: e.activation(
                out=g_sb[par][:], in_=g_sb[par][:], func=AF.Ln,
                bias=onecol[:]))
            P.op("scalar", lambda e: e.drain())
            P.op("scalar", lambda e, par=par: e.activation(
                out=g_sb[par][:], in_=g_sb[par][:], func=AF.Exp,
                scale=-1.0), "sS")
            g_sS = P.cnt["sS"]
            P.wait("vector", "sS", g_sS)
            P.wait("vector", sx_dX[0], sx_dX[1])
            P.acquire("vector", f"stg2{par}")
            P.op("vector", lambda e: e.drain())
            P.op("vector", lambda e, tc=tc, par=par: e.scalar_tensor_tensor(
                out=g_sb[par][:], in0=g_sb[par][:], scalar=1.0,
                in1=bankc(tc, 0, 512), op0=ALU.mult, op1=ALU.mult), "vS")
            P.release(f"dbank{tc}", "vS")
            P.op("vector", lambda e: e.drain())
            P.op("vector", lambda e, par=par: e.tensor_mul(
                stg2[par][:], g_sb[par][:], sbk[par][:]))
            P.op("vector", lambda e: e.drain())
            P.op("vector", lambda e, par=par: e.tensor_add(
                stg2[par][:], stg2[par][:], xbk[par][:]), "vS")
            fin_vS = P.cnt["vS"]
            P.release(f"g_sb{par}", "vS", fin_vS)
            P.release(f"sbk{par}", "vS", fin_vS)
            P.release(f"xbk{par}", "vS", fin_vS)
            P.wait("sync", "vS", fin_vS)
            P.op("sync", lambda e, dc=dc, tc=tc, par=par: e.dma_start(
                out=out_d[tc * 128:(tc + 1) * 128, dc * 512:(dc + 1) * 512],
                in_=stg2[par][:]), f"dST{par}", 16)
            P.release(f"stg2{par}", f"dST{par}")
    P.wait("sync", "dST0", P.cnt["dST0"])
    P.wait("sync", "dST1", P.cnt["dST1"])
    esD.close()

    with nc.Block() as block:
        for eng_name in ENGINES:
            def body(e, eng_name=eng_name):
                for item in P.prog[eng_name]:
                    if item[0] == "w":
                        e.wait_ge(sems[item[1]], item[2])
                    else:
                        ins = item[1](e)
                        if item[2] is not None:
                            ins.then_inc(sems[item[2][0]], item[2][1])
            getattr(block, eng_name)(body)

    es.close()
    return nc


def xn_tile_own512(xn_own, c, half):
    return xn_own[:, c * 1024 + half * 512: c * 1024 + half * 512 + 512]



def _rope_tables(positions, scale):
    inv = np.float32(ROPE_THETA) ** (
        -np.arange(Dh, dtype=np.float32) / np.float32(Dh))
    freqs = positions.astype(np.float32)[:, None] * inv[None, :]
    cos = np.cos(freqs) * np.float32(scale)
    sin = np.sin(freqs)
    sin_pm = np.concatenate([-sin[:, 0:32], sin[:, 32:64]],
                            axis=1) * np.float32(scale)
    return np.ascontiguousarray(cos), np.ascontiguousarray(sin_pm)


def _tile_w(wt, n_stripe, stripe_f):
    din, f = wt.shape
    c = din // 128
    assert f == n_stripe * stripe_f
    r = wt.reshape(c, 128, n_stripe, stripe_f).transpose(2, 1, 0, 3)
    return np.ascontiguousarray(r)


def kernel(L0_attn_hidden_states, L0_attn_Wq, L0_attn_Wk, L0_attn_Wv,
           L0_attn_Wo, L0_attn_lambda_q1, L0_attn_lambda_k1,
           L0_attn_lambda_q2, L0_attn_lambda_k2, L0_attn_subln_weight,
           L0_ffn_w_gate_up, L0_ffn_w_down):
    x = np.asarray(L0_attn_hidden_states, dtype=np.float32)
    wq = np.asarray(L0_attn_Wq, dtype=np.float32)
    wk = np.asarray(L0_attn_Wk, dtype=np.float32)
    wv = np.asarray(L0_attn_Wv, dtype=np.float32)
    wo = np.asarray(L0_attn_Wo, dtype=np.float32)
    wgu = np.asarray(L0_ffn_w_gate_up, dtype=np.float32)
    wd = np.asarray(L0_ffn_w_down, dtype=np.float32)

    perm = []
    for j in range(8):
        perm += [4 * j, 4 * j + 2, 4 * j + 1, 4 * j + 3]
    row_perm = np.concatenate([q * 64 + np.arange(64) for q in perm])
    wq_t = _tile_w(np.ascontiguousarray(wq[row_perm].T), 4, 512)
    wk_t = _tile_w(np.ascontiguousarray(wk.T), 2, 512)
    wv_t = _tile_w(np.ascontiguousarray(wv.T), 2, 512)
    wo_t = _tile_w(np.ascontiguousarray(wo.T), 4, 512)
    wgu_t = _tile_w(np.ascontiguousarray(wgu.T), 128, 128)
    wdT = np.ascontiguousarray(wd.T)
    wd_t = np.ascontiguousarray(
        wdT.reshape(4, 16, 128, 4, 512).transpose(3, 0, 2, 1, 4))

    ident = np.eye(128, dtype=np.float32)
    lq1 = np.asarray(L0_attn_lambda_q1, np.float32).reshape(1, Dh)
    lk1 = np.asarray(L0_attn_lambda_k1, np.float32).reshape(1, Dh)
    lq2 = np.asarray(L0_attn_lambda_q2, np.float32).reshape(1, Dh)
    lk2 = np.asarray(L0_attn_lambda_k2, np.float32).reshape(1, Dh)
    subw = np.asarray(L0_attn_subln_weight, np.float32).reshape(1, 2 * Dh)

    kpos = np.arange(S)
    kcos, ksin = _rope_tables(kpos, 1.0)
    in_maps = []
    qpos_all = []
    for c in range(8):
        b, h = c // 2, c % 2
        chunks = [0, 1, 2, 3, 12, 13, 14, 15] if h == 0 else list(range(4, 12))
        qpos = np.concatenate([ch * 128 + np.arange(128) for ch in chunks])
        qpos_all.append(qpos)
        xb = x[b]
        x_q = np.ascontiguousarray(xb[qpos])
        qcos, qsin = _rope_tables(qpos, SCALING)
        gk = kpos.reshape(16, 128).T
        gq = qpos.reshape(8, 128)
        m = (gk[:, None, :, None] <= gq[None, :, None, :])
        mask = np.ascontiguousarray(
            m.reshape(128, 8 * 16 * 128)).astype(ml_dtypes.bfloat16)
        in_maps.append({
            "x": np.ascontiguousarray(xb), "xq": x_q,
            "wq_t": wq_t, "wk_t": wk_t, "wv_t": wv_t,
            "wo_t": wo_t, "wgu_t": wgu_t, "wd_t": wd_t,
            "qcos": qcos, "qsin": qsin, "kcos": kcos, "ksin": ksin,
            "mask": mask, "ident": ident,
            "lq1": lq1, "lk1": lk1, "lq2": lq2, "lk2": lk2, "subw": subw,
        })

    nc = build_nc()
    res = run_bass_kernel_spmd(nc, in_maps, core_ids=list(range(8)))

    out = np.empty((4, S, D), dtype=np.float32)
    for c in range(8):
        b = c // 2
        out[b, qpos_all[c], :] = np.asarray(res.results[c]["out"])
    return out


# version 13
# speedup vs baseline: 1.0022x; 1.0022x over previous
import math
import numpy as np
import ml_dtypes
from contextlib import ExitStack

import concourse.bass as bass
import concourse.mybir as mybir
from concourse.bass_utils import run_bass_kernel_spmd

F32 = mybir.dt.float32
BF16 = mybir.dt.bfloat16
AF = mybir.ActivationFunctionType
ALU = mybir.AluOpType
AX = mybir.AxisListType

D = 2048
S = 2048
TOK = 1024
NTC = 16
NTO = 8
Dh = 64
INTER = 8192
EPS = 1e-5
LAMBDA_INIT = 0.8 - 0.6 * math.exp(-0.3 * 2)
ROPE_THETA = 10000.0
SCALING = Dh ** -0.5

ENGINES = ["sync", "gpsimd", "tensor", "vector", "scalar"]


class Plan:

    def __init__(self):
        self.prog = {e: [] for e in ENGINES}
        self.cnt = {}
        self.rel = {}

    def sem(self, name):
        self.cnt[name] = 0

    def wait(self, eng, sem, n=None):
        v = self.cnt[sem] if n is None else n
        if v > 0:
            self.prog[eng].append(("w", sem, v))

    def op(self, eng, fn, inc=None, k=1):
        tgt = None
        if inc is not None:
            self.cnt[inc] += k
            tgt = (inc, k)
        self.prog[eng].append(("o", fn, tgt))

    def acquire(self, eng, res):
        for sem, n in self.rel.get(res, ()):
            self.wait(eng, sem, n)

    def release(self, res, sem, n=None):
        self.rel[res] = [(sem, self.cnt[sem] if n is None else n)]

    def release_also(self, res, sem, n=None):
        self.rel.setdefault(res, []).append(
            (sem, self.cnt[sem] if n is None else n))


def build_nc():
    nc = bass.Bass("TRN2", target_bir_lowering=False)

    def din(name, shape, dt=F32):
        return nc.dram_tensor(name, shape, dt, kind="ExternalInput").ap()

    x_in = din("x", [S, D])
    xq_in = din("xq", [TOK, D])
    wq_t = din("wq_t", [4, 128, 16, 512])
    wk_t = din("wk_t", [2, 128, 16, 512])
    wv_t = din("wv_t", [2, 128, 16, 512])
    wo_t = din("wo_t", [4, 128, 16, 512])
    wgu_t = din("wgu_t", [128, 128, 16, 128])
    wd_t = din("wd_t", [4, 4, 128, 16, 512])
    qcos_in = din("qcos", [TOK, Dh])
    qsin_in = din("qsin", [TOK, Dh])
    kcos_in = din("kcos", [S, Dh])
    ksin_in = din("ksin", [S, Dh])
    mask_in = din("mask", [128, 8 * 2048], BF16)
    ident_in = din("ident", [128, 128])
    lq1_in = din("lq1", [1, Dh])
    lk1_in = din("lk1", [1, Dh])
    lq2_in = din("lq2", [1, Dh])
    lk2_in = din("lk2", [1, Dh])
    subw_in = din("subw", [1, 2 * Dh])

    s_dram = nc.dram_tensor("s_scr", [TOK, D], F32, kind="Internal").ap()
    negb_dram = nc.dram_tensor("negb_scr", [8, 32, 128], BF16,
                               kind="Internal").ap()
    h_dram = nc.dram_tensor("h_scr", [64, 128, 1024], BF16,
                            kind="Internal").ap()
    wd_bf = nc.dram_tensor("wd_bf", [4, 4, 128, 16, 512], BF16,
                           kind="Internal").ap()
    xn_dram = nc.dram_tensor("xn_scr", [16, 128, 16, 128], BF16,
                             kind="Internal").ap()
    wo_bf = nc.dram_tensor("wo_bf", [4, 128, 16, 512], BF16,
                           kind="Internal").ap()
    attn_dram = nc.dram_tensor("attn_scr", [TOK, D], BF16,
                               kind="Internal").ap()
    out_d = nc.dram_tensor("out", [TOK, D], F32, kind="ExternalOutput").ap()

    es = ExitStack()

    def sb(stk, name, shape, dt, side):
        return stk.enter_context(nc.sbuf_tensor(name, shape, dt, side=side))

    id_bf = sb(es, "id_bf", [128, 128], BF16, "left")
    id_f = sb(es, "id_f", [128, 128], F32, "left")
    ones_f = sb(es, "ones_f", [1, 128], F32, "left")
    ones_bf = sb(es, "ones_bf", [128, 128], BF16, "left")
    lamn_col = sb(es, "lamn_col", [128, 1], F32, "left")
    w_bcast = sb(es, "w_bcast", [128, 128], F32, "left")
    b_sb = sb(es, "b_sb", [128, 256], F32, "left")
    kdmax_f = sb(es, "kdmax_f", [128, 8], F32, "left")
    kdmax_b = sb(es, "kdmax_b", [128, 8], BF16, "left")
    small = sb(es, "small", [1, 640], F32, "left")
    epscol = sb(es, "epscol", [128, 1], F32, "left")
    onecol = sb(es, "onecol", [128, 1], F32, "left")
    h_stg = [sb(es, f"h_stg{i}", [128, 512], BF16, "left") for i in range(2)]
    stage = [sb(es, f"stage{i}", [128, 512], F32, "left") for i in range(2)]
    xn_own = sb(es, "xn_own", [128, 16 * 1024], BF16, "left")

    arena = es.enter_context(nc.psum_tensor("arena", [128, 4096], F32))

    sems = {}
    for sname in ["dT", "dID", "dX0", "dX1", "dM0", "dM1", "dSB0", "dSB1",
                  "dWB0", "dWB1", "dGU0", "dGU1", "dGU2", "dWD0", "dWD1",
                  "dXN0", "dXN1",
                  "dST0", "dST1", "dH0", "dH1", "dWC", "dAT0", "dAT1", "dAT2",
                  "dAT3", "tS", "vS", "sS"]:
        sems[sname] = es.enter_context(nc.semaphore(sname))

    P = Plan()
    for sname in sems:
        P.sem(sname)

    def bank(i, w=512):
        return arena[:, i * 512: i * 512 + w]

    def bankc(i, off, w):
        return arena[:, i * 512 + off: i * 512 + off + w]

    P.op("sync", lambda e: e.dma_start(out=id_f[:], in_=ident_in), "dT", 16)
    P.op("sync", lambda e: e.dma_start(out=small[:, 0:64], in_=lq1_in),
         "dT", 16)
    P.op("sync", lambda e: e.dma_start(out=small[:, 64:128], in_=lk1_in),
         "dT", 16)
    P.op("sync", lambda e: e.dma_start(out=small[:, 128:192], in_=lq2_in),
         "dT", 16)
    P.op("sync", lambda e: e.dma_start(out=small[:, 192:256], in_=lk2_in),
         "dT", 16)
    P.op("sync", lambda e: e.dma_start(out=small[:, 256:384], in_=subw_in),
         "dT", 16)
    P.op("gpsimd", lambda e: e.dma_start(out=id_bf[:], in_=ident_in),
         "dID", 16)


    esKVQ = ExitStack()
    k_fm = sb(esKVQ, "k_fm", [128, 8 * 2048], BF16, "left")
    v_sb = sb(esKVQ, "v_sb", [128, 8 * 16 * 132], BF16, "left")
    esP1 = ExitStack()
    xbuf = [sb(esP1, f"xbuf{i}", [128, D], F32, "left") for i in range(2)]
    xn_tm = [sb(esP1, f"xn_tm{i}", [128, D], BF16, "left") for i in range(2)]
    lncol = sb(esP1, "lncol", [128, 16], F32, "left")
    xnstg = [sb(esP1, f"xnstg{i}", [128, 2048], BF16, "left")
             for i in range(2)]

    P.op("vector", lambda e: e.memset(v_sb[:], 1.0))
    P.op("vector", lambda e: e.memset(epscol[:], EPS))
    P.op("vector", lambda e: e.memset(onecol[:], 1.0))
    P.op("vector", lambda e: e.memset(ones_bf[:], 1.0))
    P.op("vector", lambda e: e.memset(ones_f[:], 1.0))
    P.op("vector", lambda e: e.drain(), "vS")
    consts_vS = P.cnt["vS"]

    pend1 = []

    def emit_p1_transposes():
        t0, par0, xn0 = pend1.pop(0)
        is_q = t0 >= 16
        tloc = t0 - 16 if is_q else t0
        if not is_q:
            P.acquire("vector", f"xnstg{t0 % 2}")
        for hb in range(2):
            P.wait("tensor", "vS", xn0)
            P.acquire("tensor", f"lnbank{par0 % 2}")
            for cc in range(8):
                c = hb * 8 + cc
                P.op("tensor", lambda e, par0=par0, c=c: e.transpose(
                    bankc(par0 % 2, (c % 8) * 64, 64).bitcast(BF16),
                    xn_tm[par0][:, c * 128:(c + 1) * 128], id_bf[:]),
                    "tS" if cc == 7 else None)
            t_b = P.cnt["tS"]
            P.wait("vector", "tS", t_b)
            for cc in range(8):
                c = hb * 8 + cc
                if is_q:
                    P.op("vector", lambda e, par0=par0, c=c, tl=tloc:
                         e.tensor_copy(
                             xn_own[:, c * 1024 + tl * 128:
                                    c * 1024 + (tl + 1) * 128],
                             bankc(par0 % 2, (c % 8) * 64, 64).bitcast(BF16)),
                         "vS" if cc == 7 else None)
                else:
                    P.op("vector", lambda e, par0=par0, c=c, t0=t0:
                         e.tensor_copy(
                             xnstg[t0 % 2][:, c * 128:(c + 1) * 128],
                             bankc(par0 % 2, (c % 8) * 64, 64).bitcast(BF16)),
                         "vS" if cc == 7 else None)
            P.release(f"lnbank{par0 % 2}", "vS")
        P.release(f"xn_tm{par0}", "tS", t_b)
        if not is_q:
            stg_vS = P.cnt["vS"]
            P.wait("sync", "vS", stg_vS)
            P.op("sync", lambda e, t0=t0: e.dma_start(
                out=xn_dram[t0].rearrange("p c t -> p (c t)"),
                in_=xnstg[t0 % 2][:]), f"dH{t0 % 2}", 16)
            P.release(f"xnstg{t0 % 2}", f"dH{t0 % 2}")
            xn_st[t0] = (f"dH{t0 % 2}", P.cnt[f"dH{t0 % 2}"])

    xsem = ["dX0", "dX1"]
    xn_st = {}
    for t in range(24):
        par = t % 2
        P.acquire("sync", f"xbuf{par}")
        P.op("sync", lambda e, t=t, par=par: e.dma_start(
            out=xbuf[par][:],
            in_=(x_in[t * 128:(t + 1) * 128, :] if t < 16 else
                 xq_in[(t - 16) * 128:(t - 15) * 128, :])),
            xsem[par], 16)
        x_dX = P.cnt[xsem[par]]

        mcol = lncol[:, par * 5: par * 5 + 1]
        sqcol = lncol[:, par * 5 + 1: par * 5 + 2]
        msq = lncol[:, par * 5 + 2: par * 5 + 3]
        vcol = lncol[:, par * 5 + 3: par * 5 + 4]
        rcol = lncol[:, par * 5 + 4: par * 5 + 5]
        P.wait("scalar", xsem[par], x_dX)
        P.acquire("scalar", f"sq{par}")
        P.op("scalar", lambda e: e.drain())
        P.op("scalar", lambda e, par=par, s=sqcol: e.activation(
            out=arena[:, 2048:4096], in_=xbuf[par][:], func=AF.Square,
            accum_out=s), "sS")
        sq_sS = P.cnt["sS"]
        P.wait("vector", xsem[par], x_dX)
        P.op("vector", lambda e: e.drain())
        P.op("vector", lambda e, par=par, m=mcol: e.reduce_sum(
            m, xbuf[par][:], axis=AX.X))
        P.op("vector", lambda e: e.drain())
        P.op("vector", lambda e, m=mcol: e.tensor_scalar_mul(m, m, 1.0 / D))
        P.op("vector", lambda e: e.drain())
        P.op("vector", lambda e, m=mcol, o=msq: e.tensor_mul(o, m, m))
        P.op("vector", lambda e: e.drain())
        P.wait("vector", "sS", sq_sS)
        P.op("vector", lambda e, s=sqcol, o=msq, v=vcol:
             e.scalar_tensor_tensor(out=v, in0=s, scalar=1.0 / D, in1=o,
                                    op0=ALU.mult, op1=ALU.subtract), "vS")
        var_vS = P.cnt["vS"]
        P.wait("scalar", "vS", var_vS)
        P.op("scalar", lambda e: e.drain())
        P.op("scalar", lambda e, v=vcol: e.activation(
            out=v, in_=v, func=AF.Ln, bias=epscol[:]))
        P.op("scalar", lambda e: e.drain())
        P.op("scalar", lambda e, v=vcol, r=rcol: e.activation(
            out=r, in_=v, func=AF.Exp, scale=-0.5), "sS")
        rstd_sS = P.cnt["sS"]
        P.release(f"sq{par}", "sS", rstd_sS)
        P.wait("vector", "sS", rstd_sS)
        P.acquire("vector", f"xn_tm{par}")
        P.op("vector", lambda e: e.drain())
        P.op("vector", lambda e, par=par, m=mcol, r=rcol: e.tensor_scalar(
            out=xn_tm[par][:], in0=xbuf[par][:], scalar1=m, scalar2=r,
            op0=ALU.subtract, op1=ALU.mult), "vS")
        xn_vS = P.cnt["vS"]
        P.release(f"xbuf{par}", "vS")
        if t == 0:
            P.wait("tensor", "dID", 16)
        pend1.append((t, par, xn_vS))
        if len(pend1) > 1:
            emit_p1_transposes()
    while pend1:
        emit_p1_transposes()
    esP1.close()
    p1_vS = P.cnt["vS"]
    p1_tS = P.cnt["tS"]

    esR = ExitStack()
    kcos_sb = sb(esR, "kcos_sb", [128, 16 * 64], F32, "right")
    ksin_sb = sb(esR, "ksin_sb", [128, 16 * 64], F32, "right")
    qcos_sb = sb(esR, "qcos_sb", [128, 8 * 64], F32, "right")
    qsin_sb = sb(esR, "qsin_sb", [128, 8 * 64], F32, "right")
    rope_t1 = sb(esR, "rope_t1", [128, 512], F32, "right")
    rope_t2 = sb(esR, "rope_t2", [128, 512], F32, "right")
    kq_tm = [sb(esR, f"kq_tm{i}", [128, 512], BF16, "right") for i in range(2)]
    xnb = [sb(esR, f"xnb{i}", [128, 2048], BF16, "right") for i in range(2)]
    wbuf = [sb(esR, f"wbuf{i}", [128, 16 * 512], BF16, "right")
            for i in range(2)]

    for tab, src in [(kcos_sb, kcos_in), (ksin_sb, ksin_in),
                     (qcos_sb, qcos_in), (qsin_sb, qsin_in)]:
        nt = src.shape[0] // 128
        P.op("sync", lambda e, t=tab, s=src, nt=nt: e.dma_start(
            out=t[:].rearrange("p (t d) -> p t d", d=Dh),
            in_=s.rearrange("(t p) d -> p t d", p=128)), "dT", 16)
    tab_dT = P.cnt["dT"]

    xnld = {}

    def load_xn_tile(i, tc):
        P.acquire("sync", f"xnb{i % 2}")
        P.wait("sync", xn_st[tc][0], xn_st[tc][1])
        P.op("sync", lambda e, i=i, tc=tc: e.dma_start(
            out=xnb[i % 2][:], in_=xn_dram[tc].rearrange("p c t -> p (c t)")),
            f"dXN{i % 2}", 16)
        xnld[i] = P.cnt[f"dXN{i % 2}"]

    def rope_pass(psrc, tm_dst, cos_sb_, sin_sb_, tc):
        cosv = cos_sb_[:, tc * 64:(tc + 1) * 64].unsqueeze(1) \
            .broadcast_to([128, 8, 64])
        sinA = sin_sb_[:, tc * 64: tc * 64 + 32].unsqueeze(1) \
            .broadcast_to([128, 8, 32])
        sinB = sin_sb_[:, tc * 64 + 32: (tc + 1) * 64].unsqueeze(1) \
            .broadcast_to([128, 8, 32])
        r3 = lambda ap: ap.rearrange("p (h d) -> p h d", h=8)
        P.op("vector", lambda e: e.drain())
        P.op("vector", lambda e, ps=psrc, cv=cosv: e.tensor_tensor(
            out=r3(rope_t1[:]), in0=r3(ps), in1=cv, op=ALU.mult))
        P.op("vector", lambda e, ps=psrc, sv=sinA: e.tensor_tensor(
            out=r3(rope_t2[:])[:, :, 0:32], in0=r3(ps)[:, :, 32:64],
            in1=sv, op=ALU.mult))
        P.op("vector", lambda e, ps=psrc, sv=sinB: e.tensor_tensor(
            out=r3(rope_t2[:])[:, :, 32:64], in0=r3(ps)[:, :, 0:32],
            in1=sv, op=ALU.mult))
        P.op("vector", lambda e: e.drain())
        P.op("vector", lambda e, d=tm_dst: e.tensor_add(
            d, rope_t1[:], rope_t2[:]), "vS")

    for s in range(2):
        P.acquire("gpsimd", f"wbuf{s % 2}")
        P.op("gpsimd", lambda e, s=s: e.dma_start(
            out=wbuf[s % 2][:], in_=wk_t[s].rearrange("p c f -> p (c f)")),
            f"dWB{s % 2}", 16)
        P.release(f"wld{s % 2}", f"dWB{s % 2}")
    pendk = []

    def emit_kq_tr(dest, pitch):
        s0, tc0, par0, ktm0 = pendk.pop(0)
        P.wait("tensor", "vS", ktm0)
        P.acquire("tensor", f"trbank{par0}")
        for l in range(4):
            P.op("tensor", lambda e, p2=par0, l=l: e.transpose(
                bankc(p2, l * 64, 64).bitcast(BF16),
                kq_tm[p2][:, l * 128:(l + 1) * 128], id_bf[:]),
                "tS" if l == 3 else None)
        tr_tS = P.cnt["tS"]
        P.wait("vector", "tS", tr_tS)
        for l in range(4):
            slab = s0 * 4 + l
            P.op("vector", lambda e, p2=par0, l=l, slab=slab, tc0=tc0,
                 d=dest, pi=pitch: e.tensor_copy(
                     d[:, slab * pi + tc0 * 128:
                       slab * pi + (tc0 + 1) * 128],
                     bankc(p2, l * 64, 64).bitcast(BF16)),
                 "vS" if l == 3 else None)
        P.release(f"trbank{par0}", "vS")
        P.release(f"kq_tm{par0}", "vS")

    load_xn_tile(0, 0)
    load_xn_tile(1, 1)
    for s in range(2):
        for tc in range(NTC):
            it = s * NTC + tc
            pj = 2 + it % 2
            par2 = it % 2
            P.acquire("tensor", f"kvq_bank{pj}")
            P.wait("tensor", P.rel[f"wld{s % 2}"][0][0], P.rel[f"wld{s % 2}"][0][1])
            P.wait("tensor", f"dXN{it % 2}", xnld[it])
            for c in range(16):
                P.op("tensor", lambda e, c=c, pj=pj, it=it, s=s: e.matmul(
                    bank(pj), xnb[it % 2][:, c * 128:(c + 1) * 128],
                    wbuf[s % 2][:, c * 512:(c + 1) * 512],
                    start=(c == 0), stop=(c == 15)),
                    "tS" if c == 15 else None)
            mm_tS = P.cnt["tS"]
            P.release(f"xnb{it % 2}", "tS", mm_tS)
            if it + 2 < 64:
                load_xn_tile(it + 2, (it + 2) % NTC)
            if tc == NTC - 1:
                P.release(f"wbuf{s % 2}", "tS", mm_tS)
            P.wait("vector", "tS", mm_tS)
            P.wait("vector", "dT", tab_dT)
            P.acquire("vector", f"kq_tm{par2}")
            rope_pass(bank(pj), kq_tm[par2][:], kcos_sb, ksin_sb, tc)
            P.release(f"kvq_bank{pj}", "vS")
            pendk.append((s, tc, par2, P.cnt["vS"]))
            if len(pendk) > 1:
                emit_kq_tr(k_fm, 2048)
    while pendk:
        emit_kq_tr(k_fm, 2048)

    for s in range(2):
        P.acquire("gpsimd", f"wbuf{s % 2}")
        P.op("gpsimd", lambda e, s=s: e.dma_start(
            out=wbuf[s % 2][:], in_=wv_t[s].rearrange("p c f -> p (c f)")),
            f"dWB{s % 2}", 16)
        P.release(f"wld{s % 2}", f"dWB{s % 2}")
    for s in range(2):
        for tc in range(NTC):
            it = 32 + s * NTC + tc
            pj = 2 + it % 2
            P.acquire("tensor", f"kvq_bank{pj}")
            P.wait("tensor", P.rel[f"wld{s % 2}"][0][0], P.rel[f"wld{s % 2}"][0][1])
            P.wait("tensor", f"dXN{it % 2}", xnld[it])
            if it == 32:
                P.wait("tensor", "vS", consts_vS)
            for c in range(16):
                P.op("tensor", lambda e, c=c, pj=pj, it=it, s=s: e.matmul(
                    bank(pj), xnb[it % 2][:, c * 128:(c + 1) * 128],
                    wbuf[s % 2][:, c * 512:(c + 1) * 512],
                    start=(c == 0), stop=(c == 15)),
                    "tS" if c == 15 else None)
            mm_tS = P.cnt["tS"]
            P.release(f"xnb{it % 2}", "tS", mm_tS)
            if it + 2 < 64:
                load_xn_tile(it + 2, (it + 2) % NTC)
            if tc == NTC - 1:
                P.release(f"wbuf{s % 2}", "tS", mm_tS)
            P.wait("vector", "tS", mm_tS)
            for l in range(4):
                j2 = s * 4 + l
                P.op("vector", lambda e, pj=pj, l=l, j2=j2, tc=tc:
                     e.tensor_copy(
                         v_sb[:, (j2 * 16 + tc) * 132:
                              (j2 * 16 + tc) * 132 + 128],
                         bankc(pj, l * 128, 128)),
                     "vS" if l == 3 else None)
            P.release(f"kvq_bank{pj}", "vS")

    q_fm = sb(esKVQ, "q_fm", [128, 16 * 1024], BF16, "left")

    def load_q_stripe(s):
        P.acquire("gpsimd", f"wbuf{s % 2}")
        P.op("gpsimd", lambda e, s=s: e.dma_start(
            out=wbuf[s % 2][:], in_=wq_t[s].rearrange("p c f -> p (c f)")),
            f"dWB{s % 2}", 16)
        P.release(f"wld{s % 2}", f"dWB{s % 2}")

    load_q_stripe(0)
    load_q_stripe(1)
    for s in range(4):
        for tc in range(NTO):
            it = s * NTO + tc
            pj = 2 + it % 2
            par2 = it % 2
            P.acquire("tensor", f"kvq_bank{pj}")
            P.wait("tensor", P.rel[f"wld{s % 2}"][0][0], P.rel[f"wld{s % 2}"][0][1])
            for c in range(16):
                P.op("tensor", lambda e, tc=tc, c=c, pj=pj, s=s: e.matmul(
                    bank(pj),
                    xn_own[:, c * 1024 + tc * 128: c * 1024 + (tc + 1) * 128],
                    wbuf[s % 2][:, c * 512:(c + 1) * 512],
                    start=(c == 0), stop=(c == 15)),
                    "tS" if c == 15 else None)
            mm_tS = P.cnt["tS"]
            if tc == NTO - 1:
                P.release(f"wbuf{s % 2}", "tS", mm_tS)
                if s + 2 < 4:
                    load_q_stripe(s + 2)
            P.wait("vector", "tS", mm_tS)
            P.acquire("vector", f"kq_tm{par2}")
            rope_pass(bank(pj), kq_tm[par2][:], qcos_sb, qsin_sb, tc)
            P.release(f"kvq_bank{pj}", "vS")
            pendk.append((s, tc, par2, P.cnt["vS"]))
            if len(pendk) > 1:
                emit_kq_tr(q_fm, 1024)
    while pendk:
        emit_kq_tr(q_fm, 1024)
    esR.close()
    kvq_sS = P.cnt["sS"]
    kvq_vS = P.cnt["vS"]
    kvq_tS = P.cnt["tS"]

    P.wait("vector", "dT", tab_dT)
    P.op("vector", lambda e: e.drain())
    P.op("vector", lambda e: e.tensor_mul(
        small[:, 384:448], small[:, 0:64], small[:, 64:128]))
    P.op("vector", lambda e: e.tensor_mul(
        small[:, 448:512], small[:, 128:192], small[:, 192:256]))
    P.op("vector", lambda e: e.drain())
    P.op("vector", lambda e: e.reduce_sum(
        small[:, 512:514].rearrange("p (a b) -> p a b", a=2),
        small[:, 384:512].rearrange("p (a b) -> p a b", a=2), axis=AX.X))
    P.op("vector", lambda e: e.drain(), "vS")
    lam_vS = P.cnt["vS"]
    P.wait("scalar", "vS", lam_vS)
    P.op("scalar", lambda e: e.drain())
    P.op("scalar", lambda e: e.activation(
        out=small[:, 514:516], in_=small[:, 512:514], func=AF.Exp), "sS")
    lam_sS = P.cnt["sS"]
    P.wait("vector", "sS", lam_sS)
    P.op("vector", lambda e: e.drain())
    P.op("vector", lambda e: e.tensor_sub(
        small[:, 516:517], small[:, 515:516], small[:, 514:515]))
    P.op("vector", lambda e: e.drain())
    P.op("vector", lambda e: e.tensor_scalar_add(
        small[:, 516:517], small[:, 516:517], -LAMBDA_INIT), "vS")

    lamn_vS = P.cnt["vS"]
    P.wait("tensor", "sS", kvq_sS)
    P.wait("tensor", "vS", kvq_vS)
    P.wait("tensor", "dT", tab_dT)
    P.acquire("tensor", "trbank0")
    P.acquire("tensor", "trbank1")
    P.acquire("tensor", "lnbank0")
    P.acquire("tensor", "lnbank1")
    P.wait("tensor", "vS", lamn_vS)
    P.op("tensor", lambda e: e.matmul(
        bankc(0, 64, 1), ones_f[:], small[:, 516:517], start=True, stop=True))
    P.op("tensor", lambda e: e.matmul(
        bankc(1, 0, 128), ones_f[:], small[:, 256:384], start=True,
        stop=True), "tS")
    bc_tS = P.cnt["tS"]
    P.wait("vector", "tS", bc_tS)
    P.op("vector", lambda e: e.drain())
    P.op("vector", lambda e: e.tensor_copy(lamn_col[:], bankc(0, 64, 1)))
    P.op("vector", lambda e: e.tensor_scalar_mul(
        w_bcast[:], bankc(1, 0, 128), 1.0 - LAMBDA_INIT))
    P.op("vector", lambda e: e.drain(), "vS")
    bcv_vS = P.cnt["vS"]

    es5 = ExitStack()
    qa_sb = sb(es5, "qa_sb", [128, 1024], BF16, "right")
    tb2 = sb(es5, "tb2", [32, 128], BF16, "right")
    P.wait("vector", "tS", kvq_tS)
    P.op("vector", lambda e: e.drain())
    for j2 in range(8):
        P.op("vector", lambda e, j2=j2: e.reduce_max(
            kdmax_f[:, j2:j2 + 1], k_fm[:, j2 * 2048:(j2 + 1) * 2048],
            axis=AX.X, apply_absolute_value=True))
    P.op("vector", lambda e: e.drain())
    P.op("vector", lambda e: e.tensor_copy(kdmax_b[:], kdmax_f[:]))
    P.op("vector", lambda e: e.drain(), "vS")
    kd_vS = P.cnt["vS"]
    P.acquire("tensor", "kvq_bank2")
    P.acquire("tensor", "kvq_bank3")
    for slab in range(16):
        j2 = slab // 2
        P.acquire("scalar", "qa_sb")
        if slab == 0:
            P.wait("scalar", "vS", kvq_vS)
        P.op("scalar", lambda e: e.drain())
        for tc in range(NTO):
            P.op("scalar", lambda e, slab=slab, tc=tc: e.activation(
                out=qa_sb[:, tc * 128:(tc + 1) * 128],
                in_=q_fm[:, slab * 1024 + tc * 128:
                         slab * 1024 + (tc + 1) * 128],
                func=AF.Abs),
                "sS" if tc == NTO - 1 else None)
        qa_sS = P.cnt["sS"]
        P.wait("tensor", "sS", qa_sS)
        P.wait("tensor", "vS", kd_vS)
        for tc in range(NTO):
            for h in range(2):
                idx = tc * 32 + slab * 2 + h
                P.op("tensor",
                     lambda e, slab=slab, tc=tc, h=h, j2=j2, idx=idx:
                     e.matmul(
                         bankc(2 + idx // 128, idx % 128, 1),
                         qa_sb[h * 64:(h + 1) * 64,
                               tc * 128:(tc + 1) * 128],
                         kdmax_b[h * 64:(h + 1) * 64, j2:j2 + 1],
                         start=True, stop=True),
                     "tS" if (tc == NTO - 1 and h == 1) else None)
        bm_tS = P.cnt["tS"]
        P.release("qa_sb", "tS", bm_tS)
    P.wait("vector", "tS", P.cnt["tS"])
    P.op("vector", lambda e: e.drain())
    P.op("vector", lambda e: e.tensor_scalar_mul(
        b_sb[:, 0:128], bankc(2, 0, 128), -1.0))
    P.op("vector", lambda e: e.tensor_scalar_mul(
        b_sb[:, 128:256], bankc(3, 0, 128), -1.0), "vS")
    P.release("kvq_bank2", "vS")
    P.release("kvq_bank3", "vS")
    bsb_vS = P.cnt["vS"]
    for qb in range(8):
        P.wait("tensor", "vS", bsb_vS)
        P.acquire("tensor", "lnbank0")
        P.op("tensor", lambda e, qb=qb: e.transpose(
            arena[0:32, 0:128], b_sb[:, qb * 32:(qb + 1) * 32], id_f[:]),
            "tS")
        ntr_tS = P.cnt["tS"]
        P.wait("vector", "tS", ntr_tS)
        P.acquire("vector", "tb2")
        P.op("vector", lambda e: e.drain())
        P.op("vector", lambda e: e.tensor_copy(tb2[:], arena[0:32, 0:128]),
             "vS")
        P.release("lnbank0", "vS")
        ntb_vS = P.cnt["vS"]
        P.wait("sync", "vS", ntb_vS)
        P.op("sync", lambda e, qb=qb: e.dma_start(
            out=negb_dram[qb], in_=tb2[:]), "dST0", 16)
        P.release("tb2", "dST0")
    negb_st = P.cnt["dST0"]
    es5.close()
    p5_vS = P.cnt["vS"]
    p5_sS = P.cnt["sS"]

    esA1 = ExitStack()
    e_sb = [sb(esA1, f"e_sb{g}", [128, 2048], BF16, "right") for g in range(4)]
    mask_b = sb(esA1, "mask_b", [128, 2048], BF16, "right")
    negb_b = sb(esA1, "negb_b", [128, 4096], BF16, "right")
    cmb = sb(esA1, "cmb", [128, 16], F32, "right")
    tsc_sb = sb(esA1, "tsc_sb", [128, 128], F32, "right")
    attnf_sb = [sb(esA1, f"attnf{m}", [128, 128], F32, "right")
                for m in range(2)]
    attnb_sb = [[sb(esA1, f"attnb{s}_{m}", [128, 128], BF16, "right")
                 for m in range(2)] for s in range(4)]
    sqd2 = sb(esA1, "sqd2", [128, 128], F32, "right")
    u1_sb = sb(esA1, "u1_sb", [128, 264], F32, "right")
    wg_b = [sb(esA1, f"wg_b{i}", [128, 2048], BF16, "right") for i in range(3)]
    wu_b = [sb(esA1, f"wu_b{i}", [128, 2048], BF16, "right") for i in range(3)]
    sg_sb = [sb(esA1, f"sg_sb{i}", [128, 512], F32, "right") for i in range(2)]

    P.op("vector", lambda e: e.memset(negb_b[:], 0.0))
    P.op("vector", lambda e: e.drain(), "vS")
    nbz_vS = P.cnt["vS"]

    ffn = {"i": 0}

    def emit_ffn_chunk():
        i = ffn["i"]
        if i >= 256:
            return
        ffn["i"] = i + 1
        if i % 12 == 0 and i // 12 < 20:
            ic = i // 12
            if ic < 16:
                P.op("gpsimd", lambda e, ic=ic: e.dma_start(
                    out=wd_bf[ic // 4, ic % 4].rearrange("p c f -> p (c f)"),
                    in_=wd_t[ic // 4, ic % 4].rearrange("p c f -> p (c f)")),
                    "dWC", 16)
            else:
                P.op("gpsimd", lambda e, ic=ic: e.dma_start(
                    out=wo_bf[ic - 16].rearrange("p c f -> p (c f)"),
                    in_=wo_t[ic - 16].rearrange("p c f -> p (c f)")),
                    "dWC", 16)
        ig, sub = i // 4, i % 4
        par = ig % 3
        half = sub // 2
        hp = (ig * 2 + half) % 2
        if sub == 0:
            P.acquire("gpsimd", f"wg_b{par}")
            if ig == 0:
                P.wait("gpsimd", "vS", p5_vS)
            P.op("gpsimd", lambda e, ig=ig, par=par: e.dma_start(
                out=wg_b[par][:],
                in_=wgu_t[ig].rearrange("p c f -> p (c f)")), f"dGU{par}", 16)
            P.acquire("gpsimd", f"wu_b{par}")
            P.op("gpsimd", lambda e, ig=ig, par=par: e.dma_start(
                out=wu_b[par][:],
                in_=wgu_t[64 + ig].rearrange("p c f -> p (c f)")),
                f"dGU{par}", 16)
            ffn["dGU"] = (f"dGU{par}", P.cnt[f"dGU{par}"])
        if sub % 2 == 0:
            P.wait("tensor", ffn["dGU"][0], ffn["dGU"][1])
            P.acquire("tensor", "ffnb6")
            for c in range(16):
                P.op("tensor", lambda e, c=c, half=half, par=par:
                     e.matmul(bankc(6, 0, 512),
                              wg_b[par][:, c * 128:(c + 1) * 128],
                              xn_own[:, c * 1024 + half * 512:
                                     c * 1024 + half * 512 + 512],
                              start=(c == 0), stop=(c == 15)),
                     "tS" if c == 15 else None)
            ffn["g_tS"] = P.cnt["tS"]
        else:
            P.acquire("tensor", "ffnb7")
            for c in range(16):
                P.op("tensor", lambda e, c=c, half=half, par=par:
                     e.matmul(bankc(7, 0, 512),
                              wu_b[par][:, c * 128:(c + 1) * 128],
                              xn_own[:, c * 1024 + half * 512:
                                     c * 1024 + half * 512 + 512],
                              start=(c == 0), stop=(c == 15)),
                     "tS" if c == 15 else None)
            gu_tS = P.cnt["tS"]
            if sub == 3:
                P.release(f"wg_b{par}", "tS", gu_tS)
                P.release(f"wu_b{par}", "tS", gu_tS)
            P.wait("scalar", "tS", gu_tS)
            P.acquire("scalar", f"sg_sb{hp}")
            P.op("scalar", lambda e: e.drain())
            P.op("scalar", lambda e, hp=hp: e.activation(
                out=sg_sb[hp][:], in_=bankc(6, 0, 512), func=AF.Exp,
                scale=-1.0))
            P.op("scalar", lambda e: e.drain())
            P.op("scalar", lambda e, hp=hp: e.activation(
                out=sg_sb[hp][:], in_=sg_sb[hp][:], func=AF.Ln,
                bias=onecol[:]))
            P.op("scalar", lambda e: e.drain())
            P.op("scalar", lambda e, hp=hp: e.activation(
                out=sg_sb[hp][:], in_=sg_sb[hp][:], func=AF.Exp,
                scale=-1.0), "sS")
            sg_sS = P.cnt["sS"]
            P.wait("vector", "sS", sg_sS)
            P.acquire("vector", f"h_stg{hp}")
            P.op("vector", lambda e: e.drain())
            P.op("vector", lambda e, hp=hp: e.scalar_tensor_tensor(
                out=sg_sb[hp][:], in0=sg_sb[hp][:], scalar=1.0,
                in1=bankc(6, 0, 512), op0=ALU.mult, op1=ALU.mult), "vS")
            sm_vS = P.cnt["vS"]
            P.release("ffnb6", "vS", sm_vS)
            P.op("vector", lambda e: e.drain())
            P.op("vector", lambda e, hp=hp: e.scalar_tensor_tensor(
                out=h_stg[hp][:], in0=sg_sb[hp][:], scalar=1.0,
                in1=bankc(7, 0, 512), op0=ALU.mult, op1=ALU.mult), "vS")
            hm_vS = P.cnt["vS"]
            P.release("ffnb7", "vS", hm_vS)
            P.release(f"sg_sb{hp}", "vS", hm_vS)
            P.wait("sync", "vS", hm_vS)
            P.op("sync", lambda e, ig=ig, half=half, hp=hp: e.dma_start(
                out=h_dram[ig, :, half * 512:half * 512 + 512],
                in_=h_stg[hp][:]), f"dH{hp}", 16)
            P.release(f"h_stg{hp}", f"dH{hp}")

    pend = []
    pend_av = []
    LAG = 3
    LIM = [5, 6, 7, 8, 13, 14, 15, 16]

    def emit_avA(j, LQ):
        kq0, nk0, mvS = pend_av.pop(0)
        P.wait("tensor", "vS", mvS)
        if kq0 == 0:
            P.acquire("tensor", "ubankA")
        for kk in range(nk0):
            kb = kq0 * 4 + kk
            for g in range(2):
                P.op("tensor", lambda e, g=g, kb=kb, j=j, LQ=LQ: e.matmul(
                    bankc(4 + g, 0, 129),
                    e_sb[g][:, kb * 128:(kb + 1) * 128],
                    v_sb[:, (j * 16 + kb) * 132:
                         (j * 16 + kb) * 132 + 129],
                    start=(kb == 0), stop=(kb == LQ - 1)),
                    "tS" if (kb == LQ - 1 and g == 1) else None)
    TOTKB = 8 * sum(LIM)
    kbctr = {"n": 0}

    def paced_ffn_chunk():
        if ffn["i"] * TOTKB < 256 * kbctr["n"]:
            emit_ffn_chunk()

    def emit_attn_store():
        qb0, j0, nrm0 = pend.pop(0)
        blk = qb0 * 8 + j0
        slot = blk % 4
        blkpar = blk % 2
        P.wait("sync", "vS", nrm0)
        for m in range(2):
            P.op("sync", lambda e, m=m, slot=slot, j0=j0, qb0=qb0:
                 e.dma_start(
                     out=attn_dram[qb0 * 128:(qb0 + 1) * 128,
                                   (2 * j0 + m) * 128:(2 * j0 + m + 1) * 128],
                     in_=attnb_sb[slot][m][:]), f"dAT{slot}", 16)
        P.release(f"attnb{slot}_0", f"dAT{slot}")
        P.release(f"attnb{slot}_1", f"dAT{slot}")

    P.wait("scalar", "vS", p5_vS)
    P.wait("sync", "vS", p5_vS)
    P.wait("sync", "dWB0", P.cnt["dWB0"])
    P.wait("sync", "dWB1", P.cnt["dWB1"])
    P.wait("sync", "dX0", P.cnt["dX0"])
    P.wait("sync", "dX1", P.cnt["dX1"])
    P.wait("sync", "vS", nbz_vS)
    P.wait("sync", "dST0", negb_st)
    for qb in range(8):
        P.acquire("sync", "mask_b")
        P.op("sync", lambda e, qb=qb: e.dma_start(
            out=mask_b[:],
            in_=mask_in[:, qb * 2048:(qb + 1) * 2048]), "dM0", 16)
        P.acquire("sync", "negb_b")
        P.op("sync", lambda e, qb=qb: e.dma_start(
            out=negb_b[0:1, :].rearrange("p (s c) -> p s c", s=32),
            in_=negb_dram[qb].unsqueeze(0)), "dM0", 16)
        P.op("sync", lambda e, qb=qb: e.dma_start(
            out=negb_b[64:65, :].rearrange("p (s c) -> p s c", s=32),
            in_=negb_dram[qb].unsqueeze(0)), "dM0", 16)
        mask_dM = ("dM0", P.cnt["dM0"])
        for j in range(8):
            LQ = LIM[qb]
            kbctr["n"] += LQ
            for kq in range((LQ + 3) // 4):
                nk = min(4, LQ - 4 * kq)
                P.acquire("tensor", "stripes")
                if qb == 0 and j == 0 and kq == 0:
                    P.wait("tensor", "vS", p5_vS)
                    P.wait("tensor", "sS", p5_sS)
                if j == 0 and kq == 0:
                    P.wait("tensor", mask_dM[0], mask_dM[1])
                for g in range(4):
                    rows = (g % 2) * 64
                    slab = 2 * j + (1 if g >= 2 else 0)
                    qhh = slab * 2 + g % 2
                    for kk in range(nk):
                        kb = kq * 4 + kk
                        P.op("tensor",
                             lambda e, g=g, rows=rows, slab=slab, kb=kb,
                             kk=kk, j=j, qb=qb: e.matmul(
                                 bankc(g, (kb % 4) * 128, 128),
                                 k_fm[rows:rows + 64,
                                      j * 2048 + kb * 128:
                                      j * 2048 + (kb + 1) * 128],
                                 q_fm[rows:rows + 64,
                                      slab * 1024 + qb * 128:
                                      slab * 1024 + (qb + 1) * 128],
                                 start=(kk == 0), stop=False))
                    P.op("tensor",
                         lambda e, g=g, rows=rows, qhh=qhh, nk=nk: e.matmul(
                             bankc(g, 0, nk * 128),
                             ones_bf[rows:rows + 64, :],
                             negb_b[rows:rows + 64,
                                    qhh * 128:(qhh + 1) * 128]
                             .unsqueeze(1).broadcast_to([64, nk, 128]),
                             start=False, stop=True),
                         "tS" if g == 3 else None)
                sc_tS = P.cnt["tS"]
                P.wait("scalar", "tS", sc_tS)
                for g in range(4):
                    P.op("scalar", lambda e, g=g, kq=kq, nk=nk:
                         e.activation(
                             out=e_sb[g][:, kq * 512:kq * 512 + nk * 128],
                             in_=bankc(g, 0, nk * 128), func=AF.Exp,
                             scale=1.0),
                         "sS")
                P.release("stripes", "sS")
                exp_sS = P.cnt["sS"]
                P.wait("vector", "sS", exp_sS)
                if j == 0 and kq == 0:
                    P.wait("vector", mask_dM[0], mask_dM[1])
                P.op("vector", lambda e: e.drain())
                for g in range(4):
                    P.op("vector", lambda e, g=g, kq=kq, nk=nk: e.tensor_mul(
                        e_sb[g][:, kq * 512:kq * 512 + nk * 128],
                        e_sb[g][:, kq * 512:kq * 512 + nk * 128],
                        mask_b[:, kq * 512:kq * 512 + nk * 128]),
                        "vS" if g == 3 else None)
                pend_av.append((kq, nk, P.cnt["vS"]))
                paced_ffn_chunk()
                if len(pend_av) > 1:
                    emit_avA(j, LQ)
            while pend_av:
                emit_avA(j, LQ)
            mask_vS = P.cnt["vS"]
            if j == 7:
                P.release("mask_b", "vS", mask_vS)
                P.release("negb_b", "tS")
            avA_tS = P.cnt["tS"]
            P.wait("vector", "tS", avA_tS)
            P.op("vector", lambda e: e.drain())
            P.op("vector", lambda e: e.tensor_copy(
                u1_sb[:, 0:129], bankc(4, 0, 129)))
            P.op("vector", lambda e: e.tensor_copy(
                u1_sb[:, 132:261], bankc(5, 0, 129)), "vS")
            u1_vS = P.cnt["vS"]
            P.wait("tensor", "vS", u1_vS)
            for kb in range(LQ):
                for g in range(2):
                    P.op("tensor", lambda e, g=g, kb=kb, j=j, LQ=LQ: e.matmul(
                        bankc(4 + g, 0, 129),
                        e_sb[2 + g][:, kb * 128:(kb + 1) * 128],
                        v_sb[:, (j * 16 + kb) * 132:
                             (j * 16 + kb) * 132 + 129],
                        start=(kb == 0), stop=(kb == LQ - 1)),
                        "tS" if (kb == LQ - 1 and g == 1) else None)
            avB_tS = P.cnt["tS"]
            paced_ffn_chunk()
            P.wait("vector", "tS", avB_tS)
            P.op("vector", lambda e: e.drain())
            P.op("vector", lambda e: e.tensor_copy(
                cmb[:, 0:1], u1_sb[:, 128:129]))
            P.op("vector", lambda e: e.tensor_copy(
                cmb[:, 1:2], u1_sb[:, 260:261]))
            P.op("vector", lambda e: e.tensor_copy(
                cmb[:, 2:3], bankc(4, 128, 1)))
            P.op("vector", lambda e: e.tensor_copy(
                cmb[:, 3:4], bankc(5, 128, 1)))
            P.op("vector", lambda e: e.drain())
            P.op("vector", lambda e: e.tensor_scalar_add(
                cmb[:, 0:4], cmb[:, 0:4], 1e-38))
            P.op("vector", lambda e: e.drain())
            P.op("vector", lambda e: e.reciprocal(cmb[:, 4:8], cmb[:, 0:4]))
            P.op("vector", lambda e: e.drain())
            P.wait("vector", "vS", lamn_vS)
            P.op("vector", lambda e: e.tensor_scalar_mul(
                cmb[:, 6:8], cmb[:, 6:8], lamn_col[:]))
            P.op("vector", lambda e: e.drain())
            for m in range(2):
                P.acquire("vector", f"attnf{m}")
                P.op("vector", lambda e, m=m: e.tensor_scalar_mul(
                    tsc_sb[:], u1_sb[:, m * 132:m * 132 + 128],
                    cmb[:, 4 + m: 5 + m]))
                P.op("vector", lambda e: e.drain())
                P.op("vector", lambda e, m=m: e.scalar_tensor_tensor(
                    out=attnf_sb[m][:], in0=bankc(4 + m, 0, 128),
                    scalar=cmb[:, 6 + m: 7 + m], in1=tsc_sb[:],
                    op0=ALU.mult, op1=ALU.add), "vS")
                P.op("vector", lambda e: e.drain())
            cmb_vS = P.cnt["vS"]
            P.wait("scalar", "vS", cmb_vS)
            P.op("scalar", lambda e: e.drain())
            for m in range(2):
                P.op("scalar", lambda e, m=m: e.activation(
                    out=sqd2[:], in_=attnf_sb[m][:], func=AF.Square,
                    accum_out=cmb[:, 8 + m: 9 + m]))
                P.op("scalar", lambda e: e.drain())
            P.op("scalar", lambda e: e.activation(
                out=cmb[:, 10:12], in_=cmb[:, 8:10], func=AF.Ln,
                scale=1.0 / 128, bias=epscol[:]))
            P.op("scalar", lambda e: e.drain())
            P.op("scalar", lambda e: e.activation(
                out=cmb[:, 12:14], in_=cmb[:, 10:12], func=AF.Exp,
                scale=-0.5), "sS")
            rstd_sS2 = P.cnt["sS"]
            P.wait("vector", "sS", rstd_sS2)
            P.op("vector", lambda e: e.drain())
            slot = (qb * 8 + j) % 4
            for m in range(2):
                P.acquire("vector", f"attnb{slot}_{m}")
                P.op("vector", lambda e, m=m, slot=slot:
                     e.scalar_tensor_tensor(
                         out=attnb_sb[slot][m][:], in0=attnf_sb[m][:],
                         scalar=cmb[:, 12 + m: 13 + m], in1=w_bcast[:],
                         op0=ALU.mult, op1=ALU.mult), "vS")
                P.release(f"attnf{m}", "vS")
            nrm_vS = P.cnt["vS"]
            P.release("ubankA", "vS", cmb_vS)
            pend.append((qb, j, nrm_vS))
            if len(pend) > LAG:
                emit_attn_store()
    while pend:
        emit_attn_store()
    while ffn["i"] < 256:
        emit_ffn_chunk()
    p6_vS = P.cnt["vS"]
    p6_sS = P.cnt["sS"]
    p6_tS = P.cnt["tS"]
    esA1.close()
    esKVQ.close()


    esO = ExitStack()
    wo_b = [sb(esO, f"wo_b{i}", [128, 16 * 512], BF16, "right")
            for i in range(2)]
    attn_b = sb(esO, "attn_b", [128, 16 * 1024], BF16, "right")
    P.wait("gpsimd", "tS", p6_tS)
    P.wait("gpsimd", "vS", p6_vS)
    for s4 in range(4):
        P.wait("gpsimd", f"dAT{s4}", P.cnt[f"dAT{s4}"])
    for s4 in range(4):
        P.wait("sync", f"dAT{s4}", P.cnt[f"dAT{s4}"])
    P.wait("sync", "vS", p6_vS)

    P.wait("gpsimd", "dWC", P.cnt["dWC"])

    def load_o_stripe(dc):
        P.acquire("gpsimd", f"wo_b{dc % 2}")
        P.op("gpsimd", lambda e, dc=dc: e.dma_start(
            out=wo_b[dc % 2][:], in_=wo_bf[dc].rearrange("p c f -> p (c f)")),
            f"dWB{dc % 2}", 16)
        P.release(f"wodma{dc % 2}", f"dWB{dc % 2}")

    load_o_stripe(0)
    load_o_stripe(1)
    for c in range(16):
        P.op("sync", lambda e, c=c: e.dma_start(
            out=attn_b[:, c * 1024:(c + 1) * 1024],
            in_=attn_dram[:, c * 128:(c + 1) * 128], transpose=True),
            "dX0", 16)
    at_dX = P.cnt["dX0"]
    for dc in range(4):
        for tc in range(NTO):
            it = dc * NTO + tc
            pj = it % 2
            P.acquire("tensor", f"kvq_bank{pj}")
            P.wait("tensor", P.rel[f"wodma{dc % 2}"][0][0],
                   P.rel[f"wodma{dc % 2}"][0][1])
            P.wait("tensor", "dX0", at_dX)
            if it == 0:
                P.wait("tensor", "vS", p6_vS)
                P.wait("tensor", "sS", p6_sS)
            for c in range(16):
                P.op("tensor", lambda e, tc=tc, c=c, pj=pj, dc=dc: e.matmul(
                    bank(pj),
                    attn_b[:, c * 1024 + tc * 128: c * 1024 + (tc + 1) * 128],
                    wo_b[dc % 2][:, c * 512:(c + 1) * 512],
                    start=(c == 0), stop=(c == 15)),
                    "tS" if c == 15 else None)
            mm_tS = P.cnt["tS"]
            if tc == NTO - 1:
                P.release(f"wo_b{dc % 2}", "tS", mm_tS)
                if dc + 2 < 4:
                    load_o_stripe(dc + 2)
            P.wait("vector", "tS", mm_tS)
            P.acquire("vector", f"stage{it % 2}")
            P.op("vector", lambda e, pj=pj, it=it: e.tensor_copy(
                stage[it % 2][:], bank(pj)), "vS")
            P.release(f"kvq_bank{pj}", "vS")
            st_vS = P.cnt["vS"]
            P.wait("sync", "vS", st_vS)
            P.op("sync", lambda e, it=it, tc=tc, dc=dc: e.dma_start(
                out=s_dram[tc * 128:(tc + 1) * 128, dc * 512:(dc + 1) * 512],
                in_=stage[it % 2][:]), f"dST{it % 2}", 16)
            P.release(f"stage{it % 2}", f"dST{it % 2}")
    esO.close()
    p7_vS = P.cnt["vS"]
    p7_tS = P.cnt["tS"]
    s_st0 = P.cnt["dST0"]
    s_st1 = P.cnt["dST1"]

    esD = ExitStack()
    wd_b = [sb(esD, f"wd_b{i}", [128, 16 * 512], BF16, "right")
            for i in range(2)]
    h_b = [sb(esD, f"h_b{i}", [128, 16 * 1024], BF16, "right")
           for i in range(2)]
    g_sb = [sb(esD, f"g_sb{i}", [128, 512], F32, "right") for i in range(2)]
    sbk = [sb(esD, f"sbk{i}", [128, 512], F32, "right") for i in range(2)]
    xbk = [sb(esD, f"xbk{i}", [128, 512], F32, "right") for i in range(2)]
    stg2 = [sb(esD, f"stg2{i}", [128, 512], F32, "right") for i in range(2)]

    P.wait("gpsimd", "vS", p7_vS)
    P.wait("gpsimd", "tS", p7_tS)
    P.wait("sync", "vS", p7_vS)
    P.wait("sync", "dWB0", P.cnt["dWB0"])
    P.wait("sync", "dWB1", P.cnt["dWB1"])
    P.wait("sync", "dST0", s_st0)
    P.wait("sync", "dST1", s_st1)
    P.wait("sync", "dH0", P.cnt["dH0"])
    P.wait("sync", "dH1", P.cnt["dH1"])
    wdh_sem = {}

    def load_wdh(iq):
        dc_, q_ = iq // 4, iq % 4
        P.acquire("gpsimd", f"wd_b{iq % 2}")
        P.op("gpsimd", lambda e, dc_=dc_, q_=q_, iq=iq: e.dma_start(
            out=wd_b[iq % 2][:],
            in_=wd_bf[dc_, q_].rearrange("p c f -> p (c f)")),
            f"dWD{iq % 2}", 16)
        P.acquire("sync", f"h_b{iq % 2}")
        P.op("sync", lambda e, q_=q_, iq=iq: e.dma_start(
            out=h_b[iq % 2][:].rearrange("p (c t) -> p c t", c=16),
            in_=h_dram[q_ * 16:(q_ + 1) * 16].rearrange("c p t -> p c t")),
            f"dSB{iq % 2}", 16)
        wdh_sem[iq] = (P.cnt[f"dWD{iq % 2}"], P.cnt[f"dSB{iq % 2}"])

    P.wait("gpsimd", "dWC", P.cnt["dWC"])
    dnq = {}
    load_wdh(0)
    load_wdh(1)
    for dc in range(4):
        for q in range(4):
            iq = dc * 4 + q
            P.wait("tensor", f"dWD{iq % 2}", wdh_sem[iq][0])
            P.wait("tensor", f"dSB{iq % 2}", wdh_sem[iq][1])
            if iq == 0:
                P.wait("tensor", "vS", p7_vS)
                P.wait("tensor", "sS", p6_sS)
            for tc in range(NTO):
                if q == 0:
                    P.acquire("tensor", f"dbank{tc}")
                for c in range(16):
                    P.op("tensor", lambda e, q=q, tc=tc, c=c, iq=iq: e.matmul(
                        bankc(tc, 0, 512),
                        h_b[iq % 2][:, c * 1024 + tc * 128:
                                    c * 1024 + (tc + 1) * 128],
                        wd_b[iq % 2][:, c * 512:(c + 1) * 512],
                        start=(q == 0 and c == 0),
                        stop=(q == 3 and c == 15)),
                        "tS" if (c == 15 and (q == 3 or tc == NTO - 1))
                        else None)
                if q == 3:
                    dnq[tc] = P.cnt["tS"]
            P.release(f"wd_b{iq % 2}", "tS")
            P.release(f"h_b{iq % 2}", "tS")
            if iq + 2 < 16:
                load_wdh(iq + 2)
        for tc in range(NTO):
            it = dc * NTO + tc
            par = it % 2
            P.acquire("sync", f"sbk{par}")
            P.acquire("sync", f"xbk{par}")
            P.op("sync", lambda e, dc=dc, tc=tc, par=par: e.dma_start(
                out=sbk[par][:], in_=s_dram[tc * 128:(tc + 1) * 128,
                                           dc * 512:(dc + 1) * 512]),
                f"dM{par}", 16)
            P.op("sync", lambda e, dc=dc, tc=tc, par=par: e.dma_start(
                out=xbk[par][:], in_=xq_in[tc * 128:(tc + 1) * 128,
                                           dc * 512:(dc + 1) * 512]),
                f"dM{par}", 16)
            sx_dX = (f"dM{par}", P.cnt[f"dM{par}"])
            P.wait("scalar", "tS", dnq[tc])
            P.acquire("scalar", f"g_sb{par}")
            P.op("scalar", lambda e: e.drain())
            P.op("scalar", lambda e, tc=tc, par=par: e.activation(
                out=g_sb[par][:], in_=bankc(tc, 0, 512), func=AF.Exp,
                scale=-1.0))
            P.op("scalar", lambda e: e.drain())
            P.op("scalar", lambda e, par=par: e.activation(
                out=g_sb[par][:], in_=g_sb[par][:], func=AF.Ln,
                bias=onecol[:]))
            P.op("scalar", lambda e: e.drain())
            P.op("scalar", lambda e, par=par: e.activation(
                out=g_sb[par][:], in_=g_sb[par][:], func=AF.Exp,
                scale=-1.0), "sS")
            g_sS = P.cnt["sS"]
            P.wait("vector", "sS", g_sS)
            P.wait("vector", sx_dX[0], sx_dX[1])
            P.acquire("vector", f"stg2{par}")
            P.op("vector", lambda e: e.drain())
            P.op("vector", lambda e, tc=tc, par=par: e.scalar_tensor_tensor(
                out=g_sb[par][:], in0=g_sb[par][:], scalar=1.0,
                in1=bankc(tc, 0, 512), op0=ALU.mult, op1=ALU.mult), "vS")
            P.release(f"dbank{tc}", "vS")
            P.op("vector", lambda e: e.drain())
            P.op("vector", lambda e, par=par: e.tensor_mul(
                stg2[par][:], g_sb[par][:], sbk[par][:]))
            P.op("vector", lambda e: e.drain())
            P.op("vector", lambda e, par=par: e.tensor_add(
                stg2[par][:], stg2[par][:], xbk[par][:]), "vS")
            fin_vS = P.cnt["vS"]
            P.release(f"g_sb{par}", "vS", fin_vS)
            P.release(f"sbk{par}", "vS", fin_vS)
            P.release(f"xbk{par}", "vS", fin_vS)
            P.wait("sync", "vS", fin_vS)
            P.op("sync", lambda e, dc=dc, tc=tc, par=par: e.dma_start(
                out=out_d[tc * 128:(tc + 1) * 128, dc * 512:(dc + 1) * 512],
                in_=stg2[par][:]), f"dST{par}", 16)
            P.release(f"stg2{par}", f"dST{par}")
    P.wait("sync", "dST0", P.cnt["dST0"])
    P.wait("sync", "dST1", P.cnt["dST1"])
    esD.close()

    with nc.Block() as block:
        for eng_name in ENGINES:
            def body(e, eng_name=eng_name):
                for item in P.prog[eng_name]:
                    if item[0] == "w":
                        e.wait_ge(sems[item[1]], item[2])
                    else:
                        ins = item[1](e)
                        if item[2] is not None:
                            ins.then_inc(sems[item[2][0]], item[2][1])
            getattr(block, eng_name)(body)

    es.close()
    return nc


def xn_tile_own512(xn_own, c, half):
    return xn_own[:, c * 1024 + half * 512: c * 1024 + half * 512 + 512]



def _rope_tables(positions, scale):
    inv = np.float32(ROPE_THETA) ** (
        -np.arange(Dh, dtype=np.float32) / np.float32(Dh))
    freqs = positions.astype(np.float32)[:, None] * inv[None, :]
    cos = np.cos(freqs) * np.float32(scale)
    sin = np.sin(freqs)
    sin_pm = np.concatenate([-sin[:, 0:32], sin[:, 32:64]],
                            axis=1) * np.float32(scale)
    return np.ascontiguousarray(cos), np.ascontiguousarray(sin_pm)


def _tile_w(wt, n_stripe, stripe_f):
    din, f = wt.shape
    c = din // 128
    assert f == n_stripe * stripe_f
    r = wt.reshape(c, 128, n_stripe, stripe_f).transpose(2, 1, 0, 3)
    return np.ascontiguousarray(r)


def kernel(L0_attn_hidden_states, L0_attn_Wq, L0_attn_Wk, L0_attn_Wv,
           L0_attn_Wo, L0_attn_lambda_q1, L0_attn_lambda_k1,
           L0_attn_lambda_q2, L0_attn_lambda_k2, L0_attn_subln_weight,
           L0_ffn_w_gate_up, L0_ffn_w_down):
    x = np.asarray(L0_attn_hidden_states, dtype=np.float32)
    wq = np.asarray(L0_attn_Wq, dtype=np.float32)
    wk = np.asarray(L0_attn_Wk, dtype=np.float32)
    wv = np.asarray(L0_attn_Wv, dtype=np.float32)
    wo = np.asarray(L0_attn_Wo, dtype=np.float32)
    wgu = np.asarray(L0_ffn_w_gate_up, dtype=np.float32)
    wd = np.asarray(L0_ffn_w_down, dtype=np.float32)

    perm = []
    for j in range(8):
        perm += [4 * j, 4 * j + 2, 4 * j + 1, 4 * j + 3]
    row_perm = np.concatenate([q * 64 + np.arange(64) for q in perm])
    wq_t = _tile_w(np.ascontiguousarray(wq[row_perm].T), 4, 512)
    wk_t = _tile_w(np.ascontiguousarray(wk.T), 2, 512)
    wv_t = _tile_w(np.ascontiguousarray(wv.T), 2, 512)
    wo_t = _tile_w(np.ascontiguousarray(wo.T), 4, 512)
    wgu_t = _tile_w(np.ascontiguousarray(wgu.T), 128, 128)
    wdT = np.ascontiguousarray(wd.T)
    wd_t = np.ascontiguousarray(
        wdT.reshape(4, 16, 128, 4, 512).transpose(3, 0, 2, 1, 4))

    ident = np.eye(128, dtype=np.float32)
    lq1 = np.asarray(L0_attn_lambda_q1, np.float32).reshape(1, Dh)
    lk1 = np.asarray(L0_attn_lambda_k1, np.float32).reshape(1, Dh)
    lq2 = np.asarray(L0_attn_lambda_q2, np.float32).reshape(1, Dh)
    lk2 = np.asarray(L0_attn_lambda_k2, np.float32).reshape(1, Dh)
    subw = np.asarray(L0_attn_subln_weight, np.float32).reshape(1, 2 * Dh)

    kpos = np.arange(S)
    kcos, ksin = _rope_tables(kpos, 1.0)
    in_maps = []
    qpos_all = []
    for c in range(8):
        b, h = c // 2, c % 2
        chunks = [0, 1, 2, 3, 12, 13, 14, 15] if h == 0 else list(range(4, 12))
        qpos = np.concatenate([ch * 128 + np.arange(128) for ch in chunks])
        qpos_all.append(qpos)
        xb = x[b]
        x_q = np.ascontiguousarray(xb[qpos])
        qcos, qsin = _rope_tables(qpos, SCALING)
        gk = kpos.reshape(16, 128).T
        gq = qpos.reshape(8, 128)
        m = (gk[:, None, :, None] <= gq[None, :, None, :])
        mask = np.ascontiguousarray(
            m.reshape(128, 8 * 16 * 128)).astype(ml_dtypes.bfloat16)
        in_maps.append({
            "x": np.ascontiguousarray(xb), "xq": x_q,
            "wq_t": wq_t, "wk_t": wk_t, "wv_t": wv_t,
            "wo_t": wo_t, "wgu_t": wgu_t, "wd_t": wd_t,
            "qcos": qcos, "qsin": qsin, "kcos": kcos, "ksin": ksin,
            "mask": mask, "ident": ident,
            "lq1": lq1, "lk1": lk1, "lq2": lq2, "lk2": lk2, "subw": subw,
        })

    nc = build_nc()
    res = run_bass_kernel_spmd(nc, in_maps, core_ids=list(range(8)))

    out = np.empty((4, S, D), dtype=np.float32)
    for c in range(8):
        b = c // 2
        out[b, qpos_all[c], :] = np.asarray(res.results[c]["out"])
    return out
